# version 18
# baseline (speedup 1.0000x reference)
import numpy as np
import ml_dtypes

import concourse.bass as bass
import concourse.mybir as mybir
from concourse.bass_utils import run_bass_kernel_spmd

F32 = mybir.dt.float32
BF16 = mybir.dt.bfloat16
AF = mybir.ActivationFunctionType
ALU = mybir.AluOpType
AX = mybir.AxisListType

D = 1024
NFC = 8
HEADS = 16
HD = 64
CH = 64
FFI = 2816
NIC = FFI // 128
SCALE = HD ** -0.5
EPS = 1e-5


class Prog:

    ENGS = ("pe", "dve", "act", "pool", "sp")

    NSLOT = 4

    def __init__(self):
        self.sems = ["pe", "dve", "act_c", "pool_c"]
        for q in ("sp_q", "act_q", "pool_q"):
            for s in range(self.NSLOT):
                self.sems.append(f"{q}{s}")
        self.count = {s: 0 for s in self.sems}
        self.dmaidx = {"sp_q": 0, "act_q": 0, "pool_q": 0}
        self.stream = {e: [] for e in self.ENGS}
        self.writer = {}
        self.readers = {}
        self.pe_pending = False

    def _sem_for(self, eng, dma):
        if eng == "act":
            return "act_q" if dma else "act_c"
        if eng == "pool":
            return "pool_q" if dma else "pool_c"
        if eng == "sp":
            return "sp_q"
        return eng

    def op(self, eng, emit, reads=(), writes=(), dma=False, inc=True, wclaim=()):
        sem = self._sem_for(eng, dma)
        waits = {}
        if dma:
            q = sem
            slot = self.dmaidx[q] % self.NSLOT
            self.dmaidx[q] += 1
            sem = f"{q}{slot}"
            if self.count[sem] > 0:
                waits[sem] = self.count[sem]
        for k in reads:
            w = self.writer.get(k)
            if w:
                waits[w[0]] = max(waits.get(w[0], 0), w[1])
        for k in wclaim:
            w = self.writer.get(k)
            if w:
                waits[w[0]] = max(waits.get(w[0], 0), w[1])
            for s, t in self.readers.get(k, {}).items():
                waits[s] = max(waits.get(s, 0), t)
        for k in writes:
            w = self.writer.get(k)
            if w:
                waits[w[0]] = max(waits.get(w[0], 0), w[1])
            for s, t in self.readers.get(k, {}).items():
                waits[s] = max(waits.get(s, 0), t)
        step = 16 if dma else 1
        tok = self.count[sem] + step
        if inc:
            self.count[sem] = tok
            if eng == "pe":
                self.pe_pending = False
        else:
            assert eng == "pe" and not dma
            self.pe_pending = True
        for k in reads:
            self.readers.setdefault(k, {})[sem] = tok
        for k in writes:
            self.writer[k] = (sem, tok)
        self.stream[eng].append((waits, emit, sem if inc else None, step))

    def barrier(self):
        assert not self.pe_pending
        snap = dict(self.count)
        for e in self.ENGS:
            if e == "sp":
                sem, step = "sp_q0", 0
                self.stream[e].append((dict(snap), (lambda eng: eng.nop()), None, 0))
            else:
                sem = self._sem_for(e, False)
                self.count[sem] += 1
                self.stream[e].append(
                    (dict(snap), (lambda eng: eng.nop()), sem, 1))
        self.writer = {}
        self.readers = {}

    def emit(self, nc):
        from contextlib import ExitStack
        with ExitStack() as ctx:
            ctx.enter_context(
                nc.allow_low_precision(reason="bf16 decay/scan pipeline, budgeted"))
            semh = {}
            for s in self.sems:
                semh[s] = ctx.enter_context(nc.semaphore("sem_" + s))
            block = ctx.enter_context(nc.Block())
            eng_map = {"pe": block.tensor, "dve": block.vector, "act": block.scalar,
                       "pool": block.gpsimd, "sp": block.sync}
            for ename in self.ENGS:
                stream = self.stream[ename]
                if not stream:
                    continue
                def body(eng, stream=stream):
                    waited = {}
                    for waits, emitfn, semname, step in stream:
                        for s, v in waits.items():
                            if v > 0 and waited.get(s, 0) < v:
                                eng.wait_ge(semh[s], v)
                                waited[s] = v
                        ins = emitfn(eng)
                        if semname is not None:
                            ins.then_inc(semh[semname], step)
                eng_map[ename](body)


def build_nc(ntok, stop=None):
    NT_A = ntok // 512
    NT_B = ntok // 256
    NGC = 512 // CH

    nc = bass.Bass("TRN2", target_bir_lowering=False)

    xT = nc.dram_tensor("xT", [D, ntok], F32, kind="ExternalInput").ap()
    w_d = {nm: nc.dram_tensor(nm, [D, D], F32, kind="ExternalInput").ap()
           for nm in ("Wq", "Wk", "Wv", "Wo")}
    wf1_d = nc.dram_tensor("Wf1", [D, HD], F32, kind="ExternalInput").ap()
    wf2_d = nc.dram_tensor("Wf2", [HD, D], F32, kind="ExternalInput").ap()
    wg1_d = nc.dram_tensor("Wg1", [D, HD], F32, kind="ExternalInput").ap()
    wg2_d = nc.dram_tensor("Wg2", [HD, D], F32, kind="ExternalInput").ap()
    wmod_d = nc.dram_tensor("wmod", [D, 6 * D], F32, kind="ExternalInput").ap()
    wgate_d = nc.dram_tensor("w_gate", [FFI, D], F32, kind="ExternalInput").ap()
    wup_d = nc.dram_tensor("w_up", [FFI, D], F32, kind="ExternalInput").ap()
    wdown_d = nc.dram_tensor("w_down", [D, FFI], F32, kind="ExternalInput").ap()
    nw_col_d = nc.dram_tensor("nw_col", [128, NFC], F32, kind="ExternalInput").ap()
    c_col_d = nc.dram_tensor("c_col", [128, NFC], F32, kind="ExternalInput").ap()
    identb_d = nc.dram_tensor("identb", [128, 128], BF16, kind="ExternalInput").ap()
    mask_d = nc.dram_tensor("mask128", [128, CH], F32, kind="ExternalInput").ap()
    blkR_d = nc.dram_tensor("blkR16", [128, 128], BF16, kind="ExternalInput").ap()
    blk2_d = nc.dram_tensor("sel16", [16, 1024], BF16, kind="ExternalInput").ap()
    ones1_d = nc.dram_tensor("ones1", [128, 1], BF16, kind="ExternalInput").ap()
    onesb_d = nc.dram_tensor("onesb", [1, 128], F32, kind="ExternalInput").ap()
    zero512_d = nc.dram_tensor("zero512", [128, 512], F32, kind="ExternalInput").ap()
    zero64b_d = nc.dram_tensor("zero64b", [128, CH], BF16, kind="ExternalInput").ap()

    y_fm = nc.dram_tensor("y_fm", [D, ntok], F32, kind="ExternalOutput").ap()
    xmid = nc.dram_tensor("xmid", [D, ntok], F32, kind="Internal").ap()
    sgate = nc.dram_tensor("sgate", [FFI, D], BF16, kind="Internal").ap()
    sup = nc.dram_tensor("sup", [FFI, D], BF16, kind="Internal").ap()
    sdown = nc.dram_tensor("sdown", [D, FFI], BF16, kind="Internal").ap()

    pbR0 = nc.alloc_psum_tensor("pbR0", [128, 512], F32)
    pbR1 = nc.alloc_psum_tensor("pbR1", [128, 512], F32)
    pbO = nc.alloc_psum_tensor("pbO", [128, 512], F32)
    pbPT = nc.alloc_psum_tensor("pbPT", [128, 1024], F32)
    pbKH = nc.alloc_psum_tensor("pbKH", [128, 1024], BF16)
    pbST = nc.alloc_psum_tensor("pbST", [128, 1024], F32)

    A_ = nc.alloc_sbuf_tensor
    mask = A_("mask", [128, CH], F32)
    blkR = A_("s_blkR16", [128, 128], BF16)
    blk2 = A_("s_sel16", [16, 1024], BF16)
    ones1 = A_("s_ones1", [128, 1], BF16)
    onesb = A_("s_onesb", [1, 128], F32)
    zero64b = A_("s_zero64b", [128, CH], BF16)
    nw_col = A_("s_nw_col", [128, NFC], F32)
    c_col = A_("s_c_col", [128, NFC], F32)
    sc_col = A_("sc_col", [128, NFC], F32)
    modcol = A_("modcol", [128, 48], F32)
    colv = A_("colv", [128, 6 * NFC], F32)
    ACf32 = A_("ACf32", [128, NFC * 8], F32)
    tcol = A_("tcol", [128, 2 * NFC], F32)
    lnstA = A_("lnstA", [128, 2048], F32)
    mstat = lnstA[0:1, 0:512]
    istat_f = lnstA[0:1, 512:1024]
    vstat = lnstA[32:33, 0:512]
    lsc0 = lnstA[32:33, 512:1024]
    lsc1 = lnstA[32:33, 1024:1536]
    istat = lnstA[32:33, 1536:2048]
    rst = lnstA[64:80, 0:512]
    rs0 = lnstA[64:80, 512:1024]
    rs1 = lnstA[64:80, 1024:1536]
    rbf = A_("rbf", [16, 512], BF16)
    rbc = A_("rbc", [128, 512], BF16)
    S_sb = A_("S_sb", [128, NFC, HD], F32)

    ARENA = 196512
    arena = A_("arena", [128, ARENA], mybir.dt.uint8)
    arena_addr = nc.lookup_mloc(arena).addr
    cur = [0]

    arena_off = {}

    def aalloc(name, shape, dtype, at=None):
        per = 1
        for s in shape[1:]:
            per *= s
        per *= mybir.dt.size(dtype)
        if at is not None:
            off = at
        else:
            off = (cur[0] + 31) // 32 * 32
            cur[0] = off + per
            assert cur[0] <= ARENA, f"arena overflow at {name}: {cur[0]}"
        h = nc.alloc_sbuf_tensor_at(name, shape, dtype, offset=arena_addr + off)
        arena_off[name] = off
        return h

    wq = aalloc("wq", [128, NFC, D], BF16)
    wk = aalloc("wk", [128, NFC, D], BF16)
    wv = aalloc("wv", [128, NFC, D], BF16)
    wo = aalloc("wo", [128, NFC, D], BF16)
    wf1 = aalloc("wf1", [128, NFC, HD], BF16)
    wf2 = aalloc("wf2", [HD, NFC, 128], BF16)
    wg1 = aalloc("wg1", [128, NFC, HD], BF16)
    wg2 = aalloc("wg2", [HD, NFC, 128], BF16)
    xt = [aalloc(f"xt{i}", [128, NFC, 512], F32) for i in range(2)]
    h_sb = aalloc("h_sb", [128, NFC, 512], BF16)
    xbs = [aalloc(f"xbs{i}", [128, 512], BF16) for i in range(2)]
    xqs = [aalloc(f"xqs{i}", [128, 512], BF16) for i in range(2)]
    t0s = aalloc("t0s", [128, 512], F32)
    t1s = aalloc("t1s", [128, 512], F32)
    tanhs = [aalloc(f"tanhs{i}", [128, 512], BF16) for i in range(2)]
    u1s = [aalloc(f"u1s{i}", [128, 512], BF16) for i in range(2)]
    zvs = [aalloc(f"zvs{i}", [128, 512], BF16) for i in range(2)]
    fs = [aalloc(f"fs{i}", [128, 512], BF16) for i in range(2)]
    u1f_sb = aalloc("u1f_sb", [HD, 512], BF16)
    u1g_sb = aalloc("u1g_sb", [HD, 512], BF16)
    g1s = [aalloc(f"g1s{i}", [128, 512], BF16) for i in range(2)]
    invA = aalloc("invA", [128, NFC, 512], BF16)
    A_sb = aalloc("A_sb", [128, NFC, 512], BF16)
    qT = aalloc("qT", [128, NFC, 512], BF16)
    kT = aalloc("kT", [128, NFC, 512], BF16)
    tg_sb = aalloc("tg_sb", [128, NFC, 512], BF16)
    v_tok = aalloc("v_tok", [128, 4, D], BF16)
    og = aalloc("og_al", [128, NFC, 512], BF16, at=arena_off["invA"])
    ACbf = aalloc("ACbf", [128, NFC, NGC], BF16)
    Sbf = aalloc("ln.t0", [128, NFC, HD], BF16, at=arena_off["t0s"] + 1024)
    khtok_tile = aalloc("khtok_tile", [128, 4, D], BF16, at=arena_off["A_sb"])
    ptm = aalloc("ptm", [128, 2, NFC, CH], BF16, at=arena_off["xqs0"])
    sqs = aalloc("sqs", [128, NFC, CH], BF16, at=arena_off["t0s"])
    obf = aalloc("ln.t1", [128, NFC, CH], BF16, at=arena_off["t1s"])
    khT_pair = aalloc("khT_pair", [128, NFC, 128], BF16, at=arena_off["xbs0"])
    phaseA_end = cur[0]

    phaseA_full = cur[0]
    assert phaseA_full <= ARENA, phaseA_full

    cur[0] = 0
    wgT = aalloc("wgT", [128, NFC, FFI], BF16)
    wuT = aalloc("wuT", [128, NFC, FFI], BF16)
    wdT = aalloc("wdT", [128, NIC, D], BF16)
    xm = aalloc("xm", [128, NFC, 256], F32)
    xm2 = aalloc("xm2", [128, NFC, 256], F32, at=184224)
    h2_sb = aalloc("h2_sb", [128, 2, NFC, 128], BF16)
    h2tok = aalloc("h2tok", [128, D], BF16)
    h2qT = aalloc("h2qT", [128, 2, NFC, 128], BF16)
    hmqT = aalloc("hmqT", [128, 2, NIC, 128], BF16)
    xbs_b = [aalloc(f"xbsb{i}", [128, 256], BF16) for i in range(2)]
    xqs_b = [aalloc(f"xqsb{i}", [128, 256], BF16) for i in range(2)]
    t0b = aalloc("t0b", [128, 256], F32)
    t1b = aalloc("t1b", [128, 256], F32)
    tanb = aalloc("tanb", [128, 512], BF16)
    upbf = aalloc("upbf", [128, 512], BF16)
    gtbf = aalloc("gtbf", [128, 512], BF16)
    w1b = [aalloc(f"w1b{i}", [128, 512], BF16) for i in range(2)]
    QSC_B = []
    for bp in range(2):
        QSC_B.append({
            "a": aalloc(f"b{bp}_a", [128, 512], BF16),
            "t": aalloc(f"b{bp}_t", [128, 512], BF16),
            "v2": aalloc(f"b{bp}_v2", [128, 512], BF16),
            "sg": aalloc(f"b{bp}_sg", [128, 512], BF16),
            "bm": aalloc(f"b{bp}_bm", [128, 512 // 16], F32),
            "sc2": aalloc(f"b{bp}_sc2", [128, 512 // 16], F32),
            "sc2b": aalloc(f"b{bp}_sc2b", [128, 512 // 16], BF16),
        })
    phaseB_full = cur[0]
    assert phaseB_full <= ARENA, phaseB_full

    QSC_P, KEYMAP_P, WPIN_P, WPOUT_P = [], [], [], []
    for kpar in range(2):
        host = "qT" if kpar == 0 else "kT"
        base = arena_off[host]
        qsc = {
            "a": aalloc(f"qp{kpar}_a", [128, 1024], BF16, at=base),
            "t": aalloc(f"qp{kpar}_t", [128, 1024], BF16, at=base + 2048),
            "v2": aalloc(f"qp{kpar}_v2", [128, 1024], BF16, at=base + 4096),
            "sg": aalloc(f"qp{kpar}_sg", [128, 1024], BF16, at=base + 6144),
            "bm": aalloc(f"qp{kpar}_bm", [128, 64], F32,
                         at=182944 + kpar * 640),
            "sc2": aalloc(f"qp{kpar}_sc2", [128, 64], F32,
                          at=182944 + kpar * 640 + 256),
            "sc2b": aalloc(f"qp{kpar}_sc2b", [128, 64], BF16,
                           at=182944 + kpar * 640 + 512),
        }
        QSC_P.append(qsc)
        KEYMAP_P.append({"a": host, "t": host, "v2": host, "sg": host,
                         "bm": f"pst{kpar}", "sc": f"pst{kpar}",
                         "scb": f"pst{kpar}"})
        WPIN_P.append(aalloc(f"wpinp{kpar}", [128, 1024], BF16,
                             at=192416 + kpar * 2048))
        WPOUT_P.append(aalloc(f"wpoutp{kpar}", [128, 1024], BF16,
                              at=arena_off["tg_sb"] + kpar * 2048))

    p = Prog()

    def _finish_early():
        p.barrier()
        for i in range(NT_B):
            dst = y_fm[:, i * 256:(i + 1) * 256].rearrange("(fc p) t -> p fc t", p=128)
            p.op("sp", lambda e, d=dst: e.dma_start(d, xm[:]), reads=["xm0"],
                 dma=True)
        p.barrier()
        p.emit(nc)
        return nc

    class OpSink:
        def __init__(self):
            self.ops = []

        def op(self, *a, **k):
            self.ops.append((a, k))

    def zip_emit(*sinks):
        idx = 0
        while True:
            left = False
            for s in sinks:
                if idx < len(s.ops):
                    a, k = s.ops[idx]
                    p.op(*a, **k)
                    left = True
            if not left:
                break
            idx += 1

    def ladder_quant(eng2, src_key, src_ap, out_key, out_ap, width, qsc, kpfx,
                     key_map=None, eng3="dve", sink=None):
        q = sink if sink is not None else p
        nblk = width // 16
        a, t, v2, sg = qsc["a"], qsc["t"], qsc["v2"], qsc["sg"]
        bm, sc2, sc2b = qsc["bm"], qsc["sc2"], qsc["sc2b"]
        km = key_map or {}
        ka = km.get("a", kpfx + "a")
        kt = km.get("t", kpfx + "t")
        kv2 = km.get("v2", kpfx + "v2")
        ksg = km.get("sg", kpfx + "sg")
        kbm = km.get("bm", kpfx + "bm")
        ksc = km.get("sc", kpfx + "sc")
        kscb = km.get("scb", kpfx + "scb")
        src3 = src_ap.rearrange("p (n b) -> p n b", b=16)
        q.op("dve", lambda e: e.reduce_max(
            bm[:, 0:nblk].rearrange("p n -> p n ()"), src3, axis=AX.X,
            apply_absolute_value=True), reads=[src_key], writes=[kbm])
        q.op("dve", lambda e: e.tensor_scalar(
            sc2[:, 0:nblk], bm[:, 0:nblk], 1.0 / 12.0, 5e-13, ALU.mult, ALU.max),
            reads=[kbm], writes=[ksc])
        q.op("dve", lambda e: e.reciprocal(bm[:, 0:nblk], sc2[:, 0:nblk]),
             reads=[ksc], writes=[kbm])
        q.op("dve", lambda e: e.tensor_scalar(
            sc2b[:, 0:nblk], bm[:, 0:nblk], 0.5, None, ALU.mult),
            reads=[kbm], writes=[kscb])
        q.op(eng3, lambda e: e.tensor_tensor(
            a[:, 0:width].rearrange("p (n b) -> p n b", b=16), src3,
            sc2b[:, 0:nblk].rearrange("p n -> p n ()").broadcast_to([128, nblk, 16]),
            ALU.mult), reads=[src_key, kscb], writes=[ka])
        q.op("act", lambda e: e.activation(sg[:, 0:width], a[:, 0:width], AF.Sign),
             reads=[ka], writes=[ksg])
        q.op(eng2, lambda e: e.tensor_scalar(
            v2[:, 0:width], a[:, 0:width], -1.0, None, ALU.mult),
            reads=[ka], writes=[kv2])
        q.op(eng2, lambda e: e.tensor_tensor(
            a[:, 0:width], a[:, 0:width], v2[:, 0:width], ALU.max),
            reads=[ka, kv2], writes=[ka])
        q.op(eng2, lambda e: e.tensor_scalar(
            t[:, 0:width], a[:, 0:width], 0.25, None, ALU.is_gt),
            reads=[ka], writes=[kt])
        for mid in (0.75, 1.25, 1.75):
            q.op(eng2, lambda e, m=mid: e.scalar_tensor_tensor(
                t[:, 0:width], a[:, 0:width], m, t[:, 0:width], ALU.is_gt, ALU.add),
                reads=[ka, kt], writes=[kt])
        q.op(eng2, lambda e: e.tensor_scalar(
            v2[:, 0:width], a[:, 0:width], 2.5, None, ALU.is_gt),
            reads=[ka], writes=[kv2])
        q.op(eng2, lambda e: e.scalar_tensor_tensor(
            v2[:, 0:width], a[:, 0:width], 3.5, v2[:, 0:width], ALU.is_gt, ALU.add),
            reads=[ka, kv2], writes=[kv2])
        q.op(eng2, lambda e: e.scalar_tensor_tensor(
            t[:, 0:width], v2[:, 0:width], 2.0, t[:, 0:width], ALU.mult, ALU.add),
            reads=[kv2, kt], writes=[kt])
        q.op(eng2, lambda e: e.tensor_scalar(
            v2[:, 0:width], a[:, 0:width], 5.0, None, ALU.is_gt),
            reads=[ka], writes=[kv2])
        q.op(eng2, lambda e: e.scalar_tensor_tensor(
            t[:, 0:width], v2[:, 0:width], 4.0, t[:, 0:width], ALU.mult, ALU.add),
            reads=[kv2, kt], writes=[kt])
        q.op(eng3, lambda e: e.tensor_tensor(
            t[:, 0:width], t[:, 0:width], sg[:, 0:width], ALU.mult),
            reads=[kt, ksg], writes=[kt])
        q.op("dve", lambda e: e.tensor_copy(sc2b[:, 0:nblk], sc2[:, 0:nblk]),
             reads=[ksc], writes=[kscb])
        q.op(eng3, lambda e: e.tensor_tensor(
            out_ap.rearrange("p (n b) -> p n b", b=16),
            t[:, 0:width].rearrange("p (n b) -> p n b", b=16),
            sc2b[:, 0:nblk].rearrange("p n -> p n ()").broadcast_to([128, nblk, 16]),
            ALU.mult), reads=[kt, kscb], writes=[out_key])

    def rsqrt_via_sqrt(pfx, dst_ap, src_ap, scrA, scrB, rows, cols):
        p.op("act", lambda e: e.activation(scrA, src_ap, AF.Sqrt),
             reads=[pfx + "v"], writes=[pfx + "s0"])
        p.op("dve", lambda e: e.reciprocal(dst_ap, scrA),
             reads=[pfx + "s0"], writes=[pfx + "r"])
        p.op("dve", lambda e: e.tensor_mul(scrB, dst_ap, dst_ap),
             reads=[pfx + "r"], writes=[pfx + "t"])
        p.op("dve", lambda e: e.tensor_mul(scrB, scrB, src_ap),
             reads=[pfx + "t", pfx + "v"], writes=[pfx + "t"])
        p.op("dve", lambda e: e.tensor_scalar(scrB, scrB, -0.5, 1.5, ALU.mult, ALU.add),
             reads=[pfx + "t"], writes=[pfx + "t"])
        p.op("dve", lambda e: e.tensor_mul(dst_ap, dst_ap, scrB),
             reads=[pfx + "r", pfx + "t"], writes=[pfx + "r"])

    def layernorm(xbuf, xkey, T, scol_off, hout, hkey, xbp, xqp, t0x, t1x,
                  tin=None):
        for fc in range(NFC):
            pa = fc % 2
            p.op("dve", lambda e, fc=fc, pa=pa: e.tensor_copy(
                xbp[pa][:, 0:T], xbuf[:, fc, 0:T]), reads=[xkey], writes=[f"xb{pa}"])
            p.op("act", lambda e, pa=pa: e.activation(
                xqp[pa][:, 0:T], xbp[pa][:, 0:T], AF.Square),
                reads=[f"xb{pa}"], writes=[f"xq{pa}"])
            p.op("pe", lambda e, pa=pa, fc=fc: e.matmul(
                pbST[0:1, 0:T], ones1[:], xbp[pa][:, 0:T],
                start=(fc == 0), stop=(fc == NFC - 1)),
                reads=[f"xb{pa}", "ones1"], wclaim=(("ST",) if fc == 0 else ()),
                writes=[], inc=True)
            p.op("pe", lambda e, pa=pa, fc=fc: e.matmul(
                pbST[0:1, 512:512 + T], ones1[:], xqp[pa][:, 0:T],
                start=(fc == 0), stop=(fc == NFC - 1)),
                reads=[f"xq{pa}", "ones1"], wclaim=(("ST",) if fc == 0 else ()),
                writes=(["ST"] if fc == NFC - 1 else []), inc=True)
        p.op("dve", lambda e: e.tensor_scalar(
            mstat[0:1, 0:T], pbST[0:1, 0:T], 1.0 / D, None, ALU.mult),
            reads=["ST"], writes=["ln.m"])
        p.op("dve", lambda e: e.tensor_scalar(
            vstat[0:1, 0:T], pbST[0:1, 512:512 + T], 1.0 / D, EPS, ALU.mult, ALU.add),
            reads=["ST"], writes=["ln.v"])
        p.op("dve", lambda e: e.tensor_mul(lsc0[0:1, 0:T], mstat[0:1, 0:T],
                                           mstat[0:1, 0:T]),
             reads=["ln.m"], writes=["ln.m2"])
        p.op("dve", lambda e: e.tensor_sub(vstat[0:1, 0:T], vstat[0:1, 0:T],
                                           lsc0[0:1, 0:T]),
             reads=["ln.v", "ln.m2"], writes=["ln.v"])
        rsqrt_via_sqrt("ln.", istat[0:1, 0:T], vstat[0:1, 0:T],
                       lsc0[0:1, 0:T], lsc1[0:1, 0:T], 1, T)
        p.op("dve", lambda e: e.tensor_copy(istat_f[0:1, 0:T], istat[0:1, 0:T]),
             reads=["ln.r"], writes=["ln.bc2"])
        p.op("pe", lambda e: e.matmul(pbR0[:, 0:T], onesb[:], mstat[0:1, 0:T],
                                      start=True, stop=True),
             reads=["ln.m", "onesb"], writes=["R0"])
        p.op("pe", lambda e: e.matmul(pbR1[:, 0:T], onesb[:], istat_f[0:1, 0:T],
                                      start=True, stop=True),
             reads=["ln.bc2", "onesb"], writes=["R1"])
        for fc in range(NFC):
            tx, tk = (t0x, "ln.t0") if fc % 2 == 0 else (t1x, "ln.t1")
            p.op("dve", lambda e, fc=fc, tx=tx: e.tensor_sub(
                tx[:, 0:T], xbuf[:, fc, 0:T], pbR0[:, 0:T]),
                reads=[xkey, "R0"], writes=[tk])
            p.op("dve", lambda e, tx=tx: e.tensor_mul(
                tx[:, 0:T], tx[:, 0:T], pbR1[:, 0:T]),
                reads=[tk, "R1"], writes=[tk])
            p.op("dve", lambda e, fc=fc, tx=tx: e.tensor_scalar(
                hout(fc), (tin(tx) if tin else tx[:, 0:T]),
                colv[:, scol_off + 8 + fc:scol_off + 8 + fc + 1],
                colv[:, scol_off + fc:scol_off + fc + 1],
                ALU.mult, ALU.add), reads=[tk, "colv"], writes=[hkey])

    for dst, src, key in [(mask, mask_d, "mask"),
                          (blkR, blkR_d, "blkR"), (blk2, blk2_d, "blk2"),
                          (ones1, ones1_d, "ones1"), (onesb, onesb_d, "onesb"),
                          (zero64b, zero64b_d, "zero"), (nw_col, nw_col_d, "nw"),
                          (c_col, c_col_d, "c_col")]:
        p.op("sp", lambda e, d=dst, s=src: e.dma_start(d[:], s), writes=[key], dma=True)
    p.op("sp", lambda e: e.dma_start(S_sb[:].rearrange("p a b -> p (a b)"),
                                     zero512_d), writes=["S"], dma=True)
    for dst, src, key in [(wf1, wf1_d, "wf1"), (wf2, wf2_d, "wf2"),
                          (wq, w_d["Wq"], "wq"), (wk, w_d["Wk"], "wk"),
                          (wv, w_d["Wv"], "wv"), (wg1, wg1_d, "wg1"),
                          (wg2, wg2_d, "wg2"), (wo, w_d["Wo"], "wo")]:
        if key in ("wf2", "wg2"):
            rs = src.rearrange("k (fc f) -> k fc f", fc=NFC)
        else:
            rs = src.rearrange("(kc p) f -> p kc f", p=128)
        p.op("pool", lambda e, d=dst, s=rs: e.dma_start(d[:], s), writes=[key], dma=True)

    if stop == "consts":
        return _finish_early()
    p.op("act", lambda e: e.activation(tcol[:, 0:NFC], c_col[:], AF.Tanh, scale=0.5),
         reads=["c_col"], writes=["tanc"])
    p.op("dve", lambda e: e.scalar_tensor_tensor(
        tcol[:, 0:NFC], tcol[:, 0:NFC], 1.0, c_col[:], ALU.add, ALU.mult),
        reads=["tanc", "c_col"], writes=["tanc"])
    p.op("dve", lambda e: e.tensor_scalar(sc_col[:], tcol[:, 0:NFC], 0.5, None,
                                          ALU.mult), reads=["tanc"], writes=["sc_col"])

    wmst = [aalloc(f"wmst{i}", [128, NFC, 512], F32, at=arena_off[f"xt{i}"])
            for i in range(2)]
    for foc in range(12):
        pa = foc % 2
        src = wmod_d[:, foc * 512:(foc + 1) * 512].rearrange("(kc p) f -> p kc f", p=128)
        p.op("sp" if pa == 0 else "act", lambda e, s=src, pa=pa: e.dma_start(
            wmst[pa][:], s), writes=[f"xt{pa}"], dma=True)
        for sub in range(4):
            col = foc * 4 + sub
            for kc in range(NFC):
                p.op("pe", lambda e, pa=pa, sub=sub, kc=kc, col=col: e.matmul(
                    pbPT[:, col:col + 1], wmst[pa][:, kc, sub * 128:(sub + 1) * 128],
                    sc_col[:, kc:kc + 1], start=(kc == 0), stop=(kc == NFC - 1)),
                    reads=[f"xt{pa}", "sc_col"], wclaim=(("PT",) if kc == 0 else ()),
                    writes=(["PT"] if kc == NFC - 1 else []), inc=(kc == NFC - 1))
    p.op("dve", lambda e: e.tensor_copy(modcol[:], pbPT[:, 0:48]),
         reads=["PT"], writes=["modcol"])
    p.op("dve", lambda e: e.tensor_copy(colv[:, 0:8], modcol[:, 0:8]),
         reads=["modcol"], writes=["colv"])
    p.op("dve", lambda e: e.tensor_scalar(colv[:, 8:16], modcol[:, 8:16], 1.0, None,
                                          ALU.add), reads=["modcol"], writes=["colv"])
    p.op("dve", lambda e: e.tensor_copy(colv[:, 24:32], modcol[:, 24:32]),
         reads=["modcol"], writes=["colv"])
    p.op("dve", lambda e: e.tensor_scalar(colv[:, 32:40], modcol[:, 32:40], 1.0, None,
                                          ALU.add), reads=["modcol"], writes=["colv"])
    for co, ci in ((16, 16), (40, 40)):
        p.op("dve", lambda e, ci=ci: e.tensor_scalar(
            tcol[:, 8:16], modcol[:, ci:ci + 8], -1.0, None, ALU.mult),
            reads=["modcol"], writes=["gneg"])
        p.op("dve", lambda e, co=co, ci=ci: e.tensor_tensor(
            colv[:, co:co + 8], modcol[:, ci:ci + 8], tcol[:, 8:16], ALU.max),
            reads=["modcol", "gneg"], writes=["colv"])
        p.op("dve", lambda e, co=co: e.tensor_scalar(
            colv[:, co:co + 8], colv[:, co:co + 8], 0.25, None, ALU.max),
            reads=["colv"], writes=["colv"])
        p.op("dve", lambda e, ci=ci: e.tensor_scalar(
            tcol[:, 0:8], modcol[:, ci:ci + 8], 0.0, 2.0, ALU.is_ge, ALU.mult),
            reads=["modcol"], writes=["gsn"])
        p.op("dve", lambda e: e.tensor_scalar(tcol[:, 0:8], tcol[:, 0:8], -1.0, None,
                                              ALU.add), reads=["gsn"], writes=["gsn"])
        p.op("dve", lambda e, co=co: e.tensor_mul(colv[:, co:co + 8],
                                                  colv[:, co:co + 8], tcol[:, 0:8]),
             reads=["colv", "gsn"], writes=["colv"])

    if stop == "mod":
        return _finish_early()
    prep_tasks = []
    prep_units = []

    def _mk_prep(src_d, dst_d, nrows, K):
        pieces = [(0, 1024), (1024, 1024), (2048, 768)] if K == FFI else [(0, K)]
        for r in range(nrows):
            for (k0, kw) in pieces:
                prep_units.append((src_d, dst_d, r, k0, kw))
    _mk_prep(wgate_d, sgate, NIC, D)
    _mk_prep(wup_d, sup, NIC, D)
    _mk_prep(wdown_d, sdown, NFC, FFI)

    for _ui in range(0, len(prep_units), 2):
        pair = [(kp, prep_units[_ui + kp]) for kp in range(2)
                if _ui + kp < len(prep_units)]

        def task(pair=pair):
            for kp, (src_d, dst_d, r, k0, kw) in pair:
                srcap = src_d[r * 128:(r + 1) * 128, k0:k0 + kw]
                p.op("pool", lambda e, s=srcap, kp=kp, kw=kw: e.dma_start(
                    WPIN_P[kp][:, 0:kw], s), writes=[f"wpin{kp}"], dma=True)
            sks = []
            for kp, (src_d, dst_d, r, k0, kw) in pair:
                s = OpSink()
                ladder_quant("dve", f"wpin{kp}", WPIN_P[kp][:, 0:kw], "tg",
                             WPOUT_P[kp][:, 0:kw], kw, QSC_P[kp], "qw.",
                             key_map=KEYMAP_P[kp], sink=s)
                sks.append(s)
            zip_emit(*sks)
            for kp, (src_d, dst_d, r, k0, kw) in pair:
                dst = dst_d[r * 128:(r + 1) * 128, k0:k0 + kw]
                p.op("act", lambda e, d=dst, kp=kp, kw=kw: e.dma_start(
                    d, WPOUT_P[kp][:, 0:kw]), reads=["tg"], dma=True)
        prep_tasks.append(task)

    if stop == "prep":
        for t in prep_tasks:
            t()
        prep_tasks = []
        return _finish_early()

    def load_xtile(i):
        pa = i % 2
        src = xT[:, i * 512:(i + 1) * 512].rearrange("(fc p) t -> p fc t", p=128)
        p.op("sp", lambda e, s=src, pa=pa: e.dma_start(xt[pa][:], s),
             writes=[f"xt{pa}"], dma=True)

    def lnA(i):
        layernorm(xt[i % 2][:], f"xt{i % 2}", 512, 0,
                  lambda fc: h_sb[:, fc, 0:512], "h", xbs, xqs, t0s, t1s)

    load_xtile(0)
    lnA(0)
    for i in range(NT_A):
        pa_t = i % 2
        if i + 1 < NT_A:
            load_xtile(i + 1)
        xk = f"xt{pa_t}"
        xb = xt[pa_t]

        if stop == "A_ln":
            return _finish_early()

        for kc in range(NFC):
            p.op("pe", lambda e, kc=kc: e.matmul(
                pbR0[0:HD, 0:512], wf1[:, kc, :], h_sb[:, kc, :],
                start=(kc == 0), stop=(kc == NFC - 1)),
                reads=["h", "wf1"], wclaim=(("R0",) if kc == 0 else ()),
                writes=(["R0"] if kc == NFC - 1 else []), inc=(kc == NFC - 1))
        p.op("dve", lambda e: e.tensor_copy(u1f_sb[:, :], pbR0[0:HD, 0:512]),
             reads=["R0"], writes=["u1f"])
        for foc in range(NFC):
            pa = foc % 2
            bank, bk = (pbR0, "R0") if pa == 0 else (pbR1, "R1")
            p.op("pe", lambda e, foc=foc, b=bank: e.matmul(
                b[:, 0:512], wf2[:, foc, :], u1f_sb[:, :], start=True, stop=True),
                reads=["u1f", "wf2"], writes=[bk])
            p.op("act", lambda e, b=bank, pa=pa: e.activation(
                g1s[pa][:, :], b[:, 0:512], AF.Exp, scale=-1.0),
                reads=[bk], writes=[f"g1{pa}"])
            p.op("act", lambda e, b=bank, pa=pa: e.activation(
                fs[pa][:, :], b[:, 0:512], AF.Tanh, scale=0.5),
                reads=[bk], writes=[f"fs{pa}"])
            p.op("dve", lambda e, pa=pa: e.tensor_scalar(
                g1s[pa][:, :], g1s[pa][:, :], 1.0, None, ALU.add),
                reads=[f"g1{pa}"], writes=[f"g1{pa}"])
            p.op("dve", lambda e, pa=pa: e.tensor_scalar(
                fs[pa][:, :], fs[pa][:, :], 0.5, 0.5, ALU.mult, ALU.add),
                reads=[f"fs{pa}"], writes=[f"fs{pa}"])
            for gc in range(NGC):
                p.op("dve", lambda e, foc=foc, gc=gc, pa=pa: e.tensor_tensor_scan(
                    invA[:, foc, gc * CH:(gc + 1) * CH],
                    g1s[pa][:, gc * CH:(gc + 1) * CH], zero64b[:], 1.0,
                    ALU.mult, ALU.add),
                    reads=[f"g1{pa}", "zero"], writes=["invA"])
                p.op("dve", lambda e, foc=foc, gc=gc, pa=pa: e.tensor_tensor_scan(
                    A_sb[:, foc, gc * CH:(gc + 1) * CH],
                    fs[pa][:, gc * CH:(gc + 1) * CH], zero64b[:], 1.0,
                    ALU.mult, ALU.add),
                    reads=[f"fs{pa}", "zero"], writes=["Agrp"])
        p.op("dve", lambda e: e.tensor_copy(
            ACbf[:], A_sb[:].rearrange("p a (g c) -> p a g c", c=CH)[:, :, :, CH - 1]),
            reads=["Agrp"], writes=["AC"])
        p.op("dve", lambda e: e.tensor_copy(
            ACf32[:].rearrange("p (a g) -> p a g", g=NGC),
            A_sb[:].rearrange("p a (g c) -> p a g c", c=CH)[:, :, :, CH - 1]),
            reads=["Agrp"], writes=["ACf"])

        if stop == "A_f":
            return _finish_early()
        for (wmat, wkey, sc_buf, sckey, dbuf, dkey, zsc) in (
                (wq, "wq", A_sb, "Agrp", qT, "qT", 0.5 * SCALE),
                (wk, "wk", invA, "invA", kT, "kT", 0.5)):
            for foc in range(NFC):
                pa = foc % 2
                bank, bk = (pbR0, "R0") if pa == 0 else (pbR1, "R1")
                for kc in range(NFC):
                    p.op("pe", lambda e, foc=foc, kc=kc, b=bank, w=wmat: e.matmul(
                        b[:, 0:512], w[:, kc, foc * 128:(foc + 1) * 128],
                        h_sb[:, kc, :], start=(kc == 0), stop=(kc == NFC - 1)),
                        reads=["h", wkey], wclaim=((bk,) if kc == 0 else ()),
                        writes=([bk] if kc == NFC - 1 else []), inc=(kc == NFC - 1))
                p.op("act", lambda e, b=bank, pa=pa: e.activation(
                    tanhs[pa][:, :], b[:, 0:512], AF.Tanh, scale=0.5),
                    reads=[bk], writes=[f"tanh{pa}"])
                p.op("act", lambda e, b=bank, z=zsc, pa=pa: e.activation(
                    zvs[pa][:, :], b[:, 0:512], AF.Copy, scale=z),
                    reads=[bk], writes=[f"zv{pa}"])
                p.op("dve", lambda e, foc=foc, sb_=sc_buf, pa=pa:
                     e.scalar_tensor_tensor(
                    u1s[pa][:, :], tanhs[pa][:, :], 1.0, sb_[:, foc, :],
                    ALU.add, ALU.mult),
                    reads=[f"tanh{pa}", sckey], writes=[f"u1{pa}"])
                p.op("dve", lambda e, foc=foc, d=dbuf, pa=pa: e.tensor_mul(
                    d[:, foc, :], zvs[pa][:, :], u1s[pa][:, :]),
                    reads=[f"zv{pa}", f"u1{pa}"], writes=[dkey])

        if stop == "A_qk":
            return _finish_early()
        for sub in range(4):
            for half in range(2):
                bank, bk = (pbPT, "PT") if half == 0 else (pbO, "O")
                for kc in range(NFC):
                    p.op("pe", lambda e, sub=sub, half=half, kc=kc, b=bank: e.matmul(
                        b[:, 0:512], h_sb[:, kc, sub * 128:(sub + 1) * 128],
                        wv[:, kc, half * 512:(half + 1) * 512],
                        start=(kc == 0), stop=(kc == NFC - 1)),
                        reads=["h", "wv"], wclaim=((bk,) if kc == 0 else ()),
                        writes=([bk] if kc == NFC - 1 else []), inc=(kc == NFC - 1))
                p.op("dve", lambda e, sub=sub, half=half, b=bank: e.tensor_copy(
                    v_tok[:, sub, half * 512:(half + 1) * 512], b[:, 0:512]),
                    reads=[bk], writes=["v"])

        if stop == "A_v":
            return _finish_early()
        for kc in range(NFC):
            p.op("pe", lambda e, kc=kc: e.matmul(
                pbR0[0:HD, 0:512], wg1[:, kc, :], h_sb[:, kc, :],
                start=(kc == 0), stop=(kc == NFC - 1)),
                reads=["h", "wg1"], wclaim=(("R0",) if kc == 0 else ()),
                writes=(["R0"] if kc == NFC - 1 else []), inc=(kc == NFC - 1))
        p.op("dve", lambda e: e.tensor_copy(u1g_sb[:, :], pbR0[0:HD, 0:512]),
             reads=["R0"], writes=["u1g"])
        for foc in range(NFC):
            pa = foc % 2
            bank, bk = (pbR0, "R0") if pa == 0 else (pbR1, "R1")
            p.op("pe", lambda e, foc=foc, b=bank: e.matmul(
                b[:, 0:512], wg2[:, foc, :], u1g_sb[:, :], start=True, stop=True),
                reads=["u1g", "wg2"], writes=[bk])
            p.op("act", lambda e, foc=foc, b=bank: e.activation(
                tg_sb[:, foc, :], b[:, 0:512], AF.Tanh, scale=0.5),
                reads=[bk], writes=["tg"])

        if stop == "A_g":
            return _finish_early()
        if i + 1 < NT_A:
            lnA(i + 1)
        for pr in range(4):
            p.op("dve", lambda e, pr=pr: e.tensor_tensor(
                khT_pair[:].rearrange("p a (g c) -> p a g c", c=CH),
                kT[:, :, pr * 128:(pr + 1) * 128].rearrange(
                    "p a (g c) -> p a g c", c=CH),
                ACbf[:, :, 2 * pr:2 * pr + 2].rearrange("p a g -> p a g ()")
                .broadcast_to([128, NFC, 2, CH]), ALU.mult),
                reads=["kT", "AC"], writes=["xb0", "xb1"])
            p.op("sp", lambda e, pr=pr: e.dma_start(
                khtok_tile[:, pr, :].rearrange("p (a b) -> p a b", b=128),
                khT_pair[:], transpose=True),
                reads=["xb0", "xb1"], writes=["Agrp"], dma=True)

        for gc in range(NGC):
            gpar = gc % 2
            sub = gc // 2
            sbase = gpar * 64
            tsl = slice(gc * CH, (gc + 1) * CH)
            p.op("act", lambda e: e.activation(
                Sbf[:].rearrange("p a c -> p (a c)"),
                S_sb[:].rearrange("p a c -> p (a c)"), AF.Copy),
                reads=["S"], writes=["ln.t0"])
            for hp in range(2):
                for fc in range(NFC):
                    p.op("pe", lambda e, hp=hp, fc=fc, sbase=sbase, tsl=tsl: e.matmul(
                        pbPT[sbase:sbase + 64, hp * 512 + fc * 64:hp * 512 + (fc + 1) * 64],
                        kT[hp * 64:(hp + 1) * 64, fc, tsl],
                        qT[hp * 64:(hp + 1) * 64, fc, tsl],
                        start=(fc == 0), stop=(fc == NFC - 1),
                        tile_position=(hp * 64, sbase)),
                        reads=["kT", "qT"],
                        wclaim=(("PT",) if (hp == 0 and fc == 0) else ()),
                        writes=(["PT"] if (hp == 1 and fc == NFC - 1) else []),
                        inc=(fc == NFC - 1))
            p.op("dve", lambda e, sbase=sbase: e.tensor_tensor(
                ptm[sbase:sbase + 64, :, :, :],
                pbPT[sbase:sbase + 64, :].rearrange("p (b f c) -> p b f c", b=2, c=CH),
                mask[sbase:sbase + 64, :].rearrange("p c -> p () () c").broadcast_to(
                    [64, 2, NFC, CH]), ALU.mult),
                reads=["PT", "mask"], writes=["xq0", "xq1"])
            for hp in range(2):
                for fc in range(NFC):
                    hh = 2 * fc + hp
                    p.op("pe", lambda e, hh=hh, hp=hp, fc=fc, sbase=sbase, sub=sub:
                         e.matmul(
                        pbO[hp * 64:(hp + 1) * 64, fc * 64:(fc + 1) * 64],
                        v_tok[sbase:sbase + 64, sub, hh * 64:(hh + 1) * 64],
                        ptm[sbase:sbase + 64, hp, fc, :],
                        start=(fc == 0), stop=(fc == NFC - 1),
                        tile_position=(sbase, hp * 64)),
                        reads=["v", "xq0", "xq1"],
                        wclaim=(("O",) if (hp == 0 and fc == 0) else ()),
                        writes=(["O"] if (hp == 1 and fc == NFC - 1) else []),
                        inc=(fc == NFC - 1))
            for hp in range(2):
                for fc in range(NFC):
                    p.op("pe", lambda e, hp=hp, fc=fc, tsl=tsl: e.matmul(
                        pbR0[hp * 64:(hp + 1) * 64, fc * 64:(fc + 1) * 64],
                        Sbf[hp * 64:(hp + 1) * 64, fc, :],
                        qT[hp * 64:(hp + 1) * 64, fc, tsl],
                        start=(fc == 0), stop=(fc == NFC - 1),
                        tile_position=(hp * 64, hp * 64)),
                        reads=["ln.t0", "qT"],
                        wclaim=(("R0",) if (hp == 0 and fc == 0) else ()),
                        writes=(["R0"] if (hp == 1 and fc == NFC - 1) else []),
                        inc=(fc == NFC - 1))
            p.op("act", lambda e: e.activation(
                obf[:].rearrange("p a c -> p (a c)"), pbO[:, 0:512], AF.Copy),
                reads=["O"], writes=["ln.t1"])
            p.op("dve", lambda e: e.tensor_tensor(
                obf[:].rearrange("p a c -> p (a c)"),
                obf[:].rearrange("p a c -> p (a c)"), pbR0[:, 0:512], ALU.add),
                reads=["ln.t1", "R0"], writes=["ln.t1"])
            p.op("dve", lambda e, tsl=tsl: e.scalar_tensor_tensor(
                og[:, :, tsl], tg_sb[:, :, tsl], 1.0, obf[:, :, :],
                ALU.add, ALU.mult), reads=["tg", "ln.t1"], writes=["invA"])
            p.op("pool", lambda e, tsl=tsl: e.tensor_tensor(
                sqs[:, :, :], og[:, :, tsl], og[:, :, tsl], ALU.mult),
                reads=["invA"], writes=["ln.t0"])
            for fc in range(NFC):
                p.op("pe", lambda e, fc=fc, gc=gc: e.matmul(
                    pbST[0:16, gc * 64:(gc + 1) * 64],
                    blkR[:, fc * 16:(fc + 1) * 16],
                    sqs[:, fc, :], start=(fc == 0), stop=(fc == NFC - 1)),
                    reads=["ln.t0", "blkR"], wclaim=(("ST",) if fc == 0 else ()),
                    writes=(["ST"] if fc == NFC - 1 else []), inc=(fc == NFC - 1))
            for hp in range(2):
                for fc in range(NFC):
                    hh = 2 * fc + hp
                    p.op("pe", lambda e, hh=hh, hp=hp, fc=fc, sbase=sbase, sub=sub:
                         e.matmul(
                        pbO[hp * 64:(hp + 1) * 64, fc * 64:(fc + 1) * 64],
                        khtok_tile[sbase:sbase + 64, sub, hh * 64:(hh + 1) * 64],
                        v_tok[sbase:sbase + 64, sub, hh * 64:(hh + 1) * 64],
                        start=(fc == 0), stop=(fc == NFC - 1),
                        tile_position=(sbase, hp * 64)),
                        reads=["Agrp", "v"],
                        wclaim=(("O",) if (hp == 0 and fc == 0) else ()),
                        writes=(["O"] if (hp == 1 and fc == NFC - 1) else []),
                        inc=(fc == NFC - 1))
            p.op("pool", lambda e, gc=gc: e.tensor_tensor(
                S_sb[:, :, :], S_sb[:, :, :],
                ACf32[:].rearrange("p (a g) -> p a g", g=NGC)[:, :, gc:gc + 1]
                .broadcast_to([128, NFC, HD]), ALU.mult),
                reads=["S", "ACf"], writes=["S"])
            p.op("dve", lambda e: e.tensor_tensor(
                S_sb[:].rearrange("p a c -> p (a c)"),
                S_sb[:].rearrange("p a c -> p (a c)"), pbO[:, 0:512], ALU.add),
                reads=["S", "O"], writes=["S"])

        p.op("dve", lambda e: e.tensor_scalar(
            rst[0:16, 0:512], pbST[0:16, 0:512], 1.0 / HD, 4 * EPS, ALU.mult, ALU.add),
            reads=["ST"], writes=["rms.v"])
        rsqrt_via_sqrt("rms.", rs1[0:16, 0:512], rst[0:16, 0:512], rs0[0:16, 0:512], rs0[0:16, 0:512], 16, 512)
        p.op("dve", lambda e: e.tensor_copy(rbf[:], rs1[0:16, 0:512]),
             reads=["rms.r"], writes=["rbf"])
        for fc in range(NFC):
            pa = fc % 2
            bank, bk = (pbR0, "R0") if pa == 0 else (pbR1, "R1")
            p.op("pe", lambda e, fc=fc, b=bank: e.matmul(
                b[:, 0:512], blk2[:, fc * 128:(fc + 1) * 128], rbf[0:16, :],
                start=True, stop=True),
                reads=["rbf", "blk2"], writes=[bk])
            p.op("act", lambda e, b=bank: e.activation(
                rbc[:, :], b[:, 0:512], AF.Copy), reads=[bk], writes=["rbc"])
            p.op("dve", lambda e, fc=fc: e.scalar_tensor_tensor(
                og[:, fc, :], og[:, fc, :], nw_col[:, fc:fc + 1], rbc[:, :],
                ALU.mult, ALU.mult), reads=["invA", "nw", "rbc"], writes=["invA"])
        for foc in range(NFC):
            pa = foc % 2
            bank, bk = (pbR0, "R0") if pa == 0 else (pbR1, "R1")
            for kc in range(NFC):
                p.op("pe", lambda e, foc=foc, kc=kc, b=bank: e.matmul(
                    b[:, 0:512], wo[:, kc, foc * 128:(foc + 1) * 128], og[:, kc, :],
                    start=(kc == 0), stop=(kc == NFC - 1)),
                    reads=["invA", "wo"], wclaim=((bk,) if kc == 0 else ()),
                    writes=([bk] if kc == NFC - 1 else []), inc=(kc == NFC - 1))
            p.op("dve", lambda e, foc=foc, b=bank, xb=xb: e.scalar_tensor_tensor(
                xb[:, foc, :], b[:, 0:512], colv[:, 16 + foc:16 + foc + 1],
                xb[:, foc, :], ALU.mult, ALU.add),
                reads=[bk, "colv", xk], writes=[xk])
        dst = xmid[:, i * 512:(i + 1) * 512].rearrange("(fc p) t -> p fc t", p=128)
        p.op("sp", lambda e, d=dst, xb=xb: e.dma_start(d, xb[:]), reads=[xk], dma=True)
        nslice = (len(prep_tasks) + NT_A - 1 - i) // max(1, NT_A - i)
        for _ in range(min(nslice, len(prep_tasks))):
            prep_tasks.pop(0)()

    while prep_tasks:
        prep_tasks.pop(0)()

    p.barrier()

    for kc in range(NFC):
        p.op("act", lambda e, kc=kc: e.dma_start(
            wgT[:, kc, :], sgate[:, kc * 128:(kc + 1) * 128], transpose=True),
            writes=["wgT"], dma=True)
        p.op("act", lambda e, kc=kc: e.dma_start(
            wuT[:, kc, :], sup[:, kc * 128:(kc + 1) * 128], transpose=True),
            writes=["wuT"], dma=True)
    for ic in range(NIC):
        p.op("act", lambda e, ic=ic: e.dma_start(
            wdT[:, ic, :], sdown[:, ic * 128:(ic + 1) * 128], transpose=True),
            writes=["wdT"], dma=True)

    nch_list = []
    n0 = 0
    while n0 < FFI:
        nch_list.append((n0, min(512, FFI - n0)))
        n0 += 512

    for bp in range(2):
        p.op("pool", lambda e, bp=bp: e.memset(w1b[bp][:], 0.0),
             reads=[], writes=[f"hm{bp}"])

    xmbs = [xm, xm2]

    def lnb(i):
        xmb = xmbs[i % 2]
        xmk = f"xm{i % 2}"
        src = xmid[:, i * 256:(i + 1) * 256].rearrange("(fc p) t -> p fc t", p=128)
        p.op("sp", lambda e, s=src, xmb=xmb: e.dma_start(xmb[:], s),
             writes=[xmk], dma=True)
        layernorm(xmb[:], xmk, 256, 24,
                  lambda fc: h2_sb[:, :, fc, :],
                  "h2", xbs_b, xqs_b, t0b, t1b,
                  tin=lambda t: t[:, 0:256].rearrange("p (s b) -> p s b", b=128))

    lnb(0)
    for i in range(NT_B):
        xmb = xmbs[i % 2]
        xmk = f"xm{i % 2}"

        for sub in range(2):
            ssl = slice(sub * 128, (sub + 1) * 128)
            p.op("sp", lambda e, sub=sub: e.dma_start(
                h2tok[:].rearrange("p (a b) -> p a b", b=128),
                h2_sb[:, sub, :, :], transpose=True),
                reads=["h2"], writes=["h2tok"], dma=True)
            _sk = []
            for half in range(2):
                hsl = slice(half * 512, (half + 1) * 512)
                s = OpSink()
                ladder_quant("dve", "h2tok", h2tok[:, hsl], "h2tok",
                             h2tok[:, hsl], 512, QSC_B[half], f"qb{half}.",
                             sink=s)
                _sk.append(s)
            zip_emit(*_sk)
            p.op("act", lambda e, sub=sub: e.dma_start(
                h2qT[:, sub, :, :],
                h2tok[:].rearrange("p (a b) -> p a b", b=128), transpose=True),
                writes=["h2tok", "h2qT"], dma=True)
            for pj in range(0, len(nch_list), 2):
                pcs = []
                for bp in range(2):
                    nn0, nwd = nch_list[pj + bp]
                    gb_, gk = (pbR0, "R0") if bp == 0 else (pbO, "O")
                    ub_, uk = (pbR1, "R1") if bp == 0 else (pbPT, "PT")
                    pcs.append((bp, nn0, nwd, gb_, gk, ub_, uk))
                for bp, nn0, nwd, gb_, gk, ub_, uk in pcs:
                    for kc in range(NFC):
                        p.op("pe", lambda e, kc=kc, nn0=nn0, nwd=nwd, sub=sub,
                             gb_=gb_: e.matmul(
                            gb_[:, 0:nwd], h2qT[:, sub, kc, :],
                            wgT[:, kc, nn0:nn0 + nwd],
                            start=(kc == 0), stop=(kc == NFC - 1)),
                            reads=["h2qT", "wgT"],
                            wclaim=((gk,) if kc == 0 else ()),
                            writes=([gk] if kc == NFC - 1 else []),
                            inc=(kc == NFC - 1))
                    for kc in range(NFC):
                        p.op("pe", lambda e, kc=kc, nn0=nn0, nwd=nwd, sub=sub,
                             ub_=ub_: e.matmul(
                            ub_[:, 0:nwd], h2qT[:, sub, kc, :],
                            wuT[:, kc, nn0:nn0 + nwd],
                            start=(kc == 0), stop=(kc == NFC - 1)),
                            reads=["h2qT", "wuT"],
                            wclaim=((uk,) if kc == 0 else ()),
                            writes=([uk] if kc == NFC - 1 else []),
                            inc=(kc == NFC - 1))
                for bp, nn0, nwd, gb_, gk, ub_, uk in pcs:
                    p.op("act", lambda e, nwd=nwd, gb_=gb_: e.activation(
                        tanb[:, 0:nwd], gb_[:, 0:nwd], AF.Tanh, scale=0.5),
                        reads=[gk], writes=["tanb"])
                    p.op("act", lambda e, nwd=nwd, gb_=gb_: e.activation(
                        gtbf[:, 0:nwd], gb_[:, 0:nwd], AF.Copy, scale=0.5),
                        reads=[gk], writes=["gtbf"])
                    p.op("act", lambda e, nwd=nwd, ub_=ub_: e.activation(
                        upbf[:, 0:nwd], ub_[:, 0:nwd], AF.Copy),
                        reads=[uk], writes=["upbf"])
                    p.op("dve", lambda e, nwd=nwd, bp=bp: e.scalar_tensor_tensor(
                        w1b[bp][:, 0:nwd], tanb[:, 0:nwd], 1.0, upbf[:, 0:nwd],
                        ALU.add, ALU.mult), reads=["tanb", "upbf"],
                        writes=[f"hm{bp}"])
                    p.op("dve", lambda e, nwd=nwd, bp=bp: e.tensor_mul(
                        w1b[bp][:, 0:nwd], gtbf[:, 0:nwd], w1b[bp][:, 0:nwd]),
                        reads=["gtbf", f"hm{bp}"], writes=[f"hm{bp}"])
                sks = []
                for bp, nn0, nwd, gb_, gk, ub_, uk in pcs:
                    s = OpSink()
                    ladder_quant("dve", f"hm{bp}", w1b[bp][:, 0:nwd], f"hm{bp}",
                                 w1b[bp][:, 0:nwd], nwd, QSC_B[bp], f"qb{bp}.",
                                 sink=s)
                    sks.append(s)
                zip_emit(*sks)
                for bp, nn0, nwd, gb_, gk, ub_, uk in pcs:
                    nic0 = nn0 // 128
                    nicw = nwd // 128
                    p.op("act", lambda e, nic0=nic0, nicw=nicw, nwd=nwd, sub=sub,
                         bp=bp: e.dma_start(
                        hmqT[:, sub, nic0:nic0 + nicw, :],
                        w1b[bp][:, 0:nwd].rearrange("p (a b) -> p a b", b=128),
                        transpose=True),
                        writes=[f"hm{bp}", "hmqT"], dma=True)
        if i + 1 < NT_B:
            lnb(i + 1)
        for foc in range(NFC):
            pa = foc % 2
            bank, bk = (pbR0, "R0") if pa == 0 else (pbR1, "R1")
            for ic in range(NIC):
                p.op("pe", lambda e, foc=foc, ic=ic, b=bank: e.matmul(
                    b[:, 0:256], wdT[:, ic, foc * 128:(foc + 1) * 128],
                    hmqT[:, :, ic, :], start=(ic == 0), stop=(ic == NIC - 1)),
                    reads=["hmqT", "wdT"], wclaim=((bk,) if ic == 0 else ()),
                    writes=([bk] if ic == NIC - 1 else []), inc=(ic == NIC - 1))
            p.op("dve", lambda e, foc=foc, b=bank, xmb=xmb: e.scalar_tensor_tensor(
                xmb[:, foc, :], b[:, 0:256], colv[:, 40 + foc:40 + foc + 1],
                xmb[:, foc, :], ALU.mult, ALU.add),
                reads=[bk, "colv", xmk], writes=[xmk])
        dst = y_fm[:, i * 256:(i + 1) * 256].rearrange("(fc p) t -> p fc t", p=128)
        p.op("sp", lambda e, d=dst, xmb=xmb: e.dma_start(d, xmb[:]),
             reads=[xmk], dma=True)

    p.barrier()
    p.emit(nc)
    return nc


_NC_CACHE = {}


def _get_nc(ntok):
    if ntok not in _NC_CACHE:
        _NC_CACHE[ntok] = build_nc(ntok)
    return _NC_CACHE[ntok]


def _consts():
    mask128 = np.zeros((128, CH), np.float32)
    for pp in range(128):
        mask128[pp, pp % 64:] = 1.0
    blkR16 = np.zeros((128, 128), ml_dtypes.bfloat16)
    sel16 = np.zeros((16, 1024), ml_dtypes.bfloat16)
    for fc in range(8):
        for pp in range(128):
            j = 2 * fc + (1 if pp >= 64 else 0)
            blkR16[pp, fc * 16 + j] = 1
            sel16[j, fc * 128 + pp] = 1
    return {
        "identb": np.eye(128, dtype=ml_dtypes.bfloat16),
        "mask128": mask128,
        "blkR16": blkR16,
        "sel16": sel16,
        "ones1": np.ones((128, 1), ml_dtypes.bfloat16),
        "onesb": np.ones((1, 128), np.float32),
        "zero512": np.zeros((128, 512), np.float32),
        "zero64b": np.zeros((128, CH), ml_dtypes.bfloat16),
    }


def kernel(L0_attn_x, L0_attn_Wq, L0_attn_Wk, L0_attn_Wv, L0_attn_Wo,
           L0_attn_Wf1, L0_attn_Wf2, L0_attn_Wg1, L0_attn_Wg2,
           L0_attn_norm_weight, L0_ffn_w_gate, L0_ffn_w_up, L0_ffn_w_down,
           L0_c, L0_wmod):
    x = np.asarray(L0_attn_x)
    B, ntok, d = x.shape
    assert d == D
    nc = _get_nc(ntok)
    shared = {
        "Wq": np.ascontiguousarray(L0_attn_Wq, np.float32),
        "Wk": np.ascontiguousarray(L0_attn_Wk, np.float32),
        "Wv": np.ascontiguousarray(L0_attn_Wv, np.float32),
        "Wo": np.ascontiguousarray(L0_attn_Wo, np.float32),
        "Wf1": np.ascontiguousarray(L0_attn_Wf1, np.float32),
        "Wf2": np.ascontiguousarray(L0_attn_Wf2, np.float32),
        "Wg1": np.ascontiguousarray(L0_attn_Wg1, np.float32),
        "Wg2": np.ascontiguousarray(L0_attn_Wg2, np.float32),
        "wmod": np.ascontiguousarray(L0_wmod, np.float32),
        "w_gate": np.ascontiguousarray(L0_ffn_w_gate, np.float32),
        "w_up": np.ascontiguousarray(L0_ffn_w_up, np.float32),
        "w_down": np.ascontiguousarray(L0_ffn_w_down, np.float32),
        "nw_col": np.ascontiguousarray(
            np.asarray(L0_attn_norm_weight, np.float32).reshape(NFC, 128).T),
    }
    shared.update(_consts())
    cc = np.asarray(L0_c, np.float32)
    in_maps = []
    for b in range(B):
        m = dict(shared)
        m["xT"] = np.ascontiguousarray(x[b].T.astype(np.float32, copy=False))
        m["c_col"] = np.ascontiguousarray(cc[b].reshape(NFC, 128).T)
        in_maps.append(m)
    res = run_bass_kernel_spmd(nc, in_maps, core_ids=list(range(B)))
    out = np.stack([np.ascontiguousarray(res.results[b]["y_fm"].T)
                    for b in range(B)], axis=0)
    return out


# version 19
# speedup vs baseline: 1.0018x; 1.0018x over previous
import numpy as np
import ml_dtypes

import concourse.bass as bass
import concourse.mybir as mybir
from concourse.bass_utils import run_bass_kernel_spmd

F32 = mybir.dt.float32
BF16 = mybir.dt.bfloat16
AF = mybir.ActivationFunctionType
ALU = mybir.AluOpType
AX = mybir.AxisListType

D = 1024
NFC = 8
HEADS = 16
HD = 64
CH = 64
FFI = 2816
NIC = FFI // 128
SCALE = HD ** -0.5
EPS = 1e-5


class Prog:

    ENGS = ("pe", "dve", "act", "pool", "sp")

    NSLOT = 4

    def __init__(self):
        self.sems = ["pe", "dve", "act_c", "pool_c"]
        for q in ("sp_q", "act_q", "pool_q"):
            for s in range(self.NSLOT):
                self.sems.append(f"{q}{s}")
        self.count = {s: 0 for s in self.sems}
        self.dmaidx = {"sp_q": 0, "act_q": 0, "pool_q": 0}
        self.stream = {e: [] for e in self.ENGS}
        self.writer = {}
        self.readers = {}
        self.pe_pending = False

    def _sem_for(self, eng, dma):
        if eng == "act":
            return "act_q" if dma else "act_c"
        if eng == "pool":
            return "pool_q" if dma else "pool_c"
        if eng == "sp":
            return "sp_q"
        return eng

    def op(self, eng, emit, reads=(), writes=(), dma=False, inc=True, wclaim=()):
        sem = self._sem_for(eng, dma)
        waits = {}
        if dma:
            q = sem
            slot = self.dmaidx[q] % self.NSLOT
            self.dmaidx[q] += 1
            sem = f"{q}{slot}"
            if self.count[sem] > 0:
                waits[sem] = self.count[sem]
        for k in reads:
            w = self.writer.get(k)
            if w:
                waits[w[0]] = max(waits.get(w[0], 0), w[1])
        for k in wclaim:
            w = self.writer.get(k)
            if w:
                waits[w[0]] = max(waits.get(w[0], 0), w[1])
            for s, t in self.readers.get(k, {}).items():
                waits[s] = max(waits.get(s, 0), t)
        for k in writes:
            w = self.writer.get(k)
            if w:
                waits[w[0]] = max(waits.get(w[0], 0), w[1])
            for s, t in self.readers.get(k, {}).items():
                waits[s] = max(waits.get(s, 0), t)
        step = 16 if dma else 1
        tok = self.count[sem] + step
        if inc:
            self.count[sem] = tok
            if eng == "pe":
                self.pe_pending = False
        else:
            assert eng == "pe" and not dma
            self.pe_pending = True
        for k in reads:
            self.readers.setdefault(k, {})[sem] = tok
        for k in writes:
            self.writer[k] = (sem, tok)
        self.stream[eng].append((waits, emit, sem if inc else None, step))

    def barrier(self):
        assert not self.pe_pending
        snap = dict(self.count)
        for e in self.ENGS:
            if e == "sp":
                sem, step = "sp_q0", 0
                self.stream[e].append((dict(snap), (lambda eng: eng.nop()), None, 0))
            else:
                sem = self._sem_for(e, False)
                self.count[sem] += 1
                self.stream[e].append(
                    (dict(snap), (lambda eng: eng.nop()), sem, 1))
        self.writer = {}
        self.readers = {}

    def emit(self, nc):
        from contextlib import ExitStack
        with ExitStack() as ctx:
            ctx.enter_context(
                nc.allow_low_precision(reason="bf16 decay/scan pipeline, budgeted"))
            semh = {}
            for s in self.sems:
                semh[s] = ctx.enter_context(nc.semaphore("sem_" + s))
            block = ctx.enter_context(nc.Block())
            eng_map = {"pe": block.tensor, "dve": block.vector, "act": block.scalar,
                       "pool": block.gpsimd, "sp": block.sync}
            for ename in self.ENGS:
                stream = self.stream[ename]
                if not stream:
                    continue
                def body(eng, stream=stream):
                    waited = {}
                    for waits, emitfn, semname, step in stream:
                        for s, v in waits.items():
                            if v > 0 and waited.get(s, 0) < v:
                                eng.wait_ge(semh[s], v)
                                waited[s] = v
                        ins = emitfn(eng)
                        if semname is not None:
                            ins.then_inc(semh[semname], step)
                eng_map[ename](body)


def build_nc(ntok, stop=None):
    NT_A = ntok // 512
    NT_B = ntok // 256
    NGC = 512 // CH

    nc = bass.Bass("TRN2", target_bir_lowering=False)

    xT = nc.dram_tensor("xT", [D, ntok], F32, kind="ExternalInput").ap()
    w_d = {nm: nc.dram_tensor(nm, [D, D], F32, kind="ExternalInput").ap()
           for nm in ("Wq", "Wk", "Wv", "Wo")}
    wf1_d = nc.dram_tensor("Wf1", [D, HD], F32, kind="ExternalInput").ap()
    wf2_d = nc.dram_tensor("Wf2", [HD, D], F32, kind="ExternalInput").ap()
    wg1_d = nc.dram_tensor("Wg1", [D, HD], F32, kind="ExternalInput").ap()
    wg2_d = nc.dram_tensor("Wg2", [HD, D], F32, kind="ExternalInput").ap()
    wmod_d = nc.dram_tensor("wmod", [D, 6 * D], F32, kind="ExternalInput").ap()
    wgate_d = nc.dram_tensor("w_gate", [FFI, D], F32, kind="ExternalInput").ap()
    wup_d = nc.dram_tensor("w_up", [FFI, D], F32, kind="ExternalInput").ap()
    wdown_d = nc.dram_tensor("w_down", [D, FFI], F32, kind="ExternalInput").ap()
    nw_col_d = nc.dram_tensor("nw_col", [128, NFC], F32, kind="ExternalInput").ap()
    c_col_d = nc.dram_tensor("c_col", [128, NFC], F32, kind="ExternalInput").ap()
    identb_d = nc.dram_tensor("identb", [128, 128], BF16, kind="ExternalInput").ap()
    mask_d = nc.dram_tensor("mask128", [128, CH], F32, kind="ExternalInput").ap()
    blkR_d = nc.dram_tensor("blkR16", [128, 128], BF16, kind="ExternalInput").ap()
    blk2_d = nc.dram_tensor("sel16", [16, 1024], BF16, kind="ExternalInput").ap()
    ones1_d = nc.dram_tensor("ones1", [128, 1], BF16, kind="ExternalInput").ap()
    onesb_d = nc.dram_tensor("onesb", [1, 128], F32, kind="ExternalInput").ap()
    zero512_d = nc.dram_tensor("zero512", [128, 512], F32, kind="ExternalInput").ap()
    zero64b_d = nc.dram_tensor("zero64b", [128, CH], BF16, kind="ExternalInput").ap()

    y_fm = nc.dram_tensor("y_fm", [D, ntok], F32, kind="ExternalOutput").ap()
    xmid = nc.dram_tensor("xmid", [D, ntok], F32, kind="Internal").ap()
    sgate = nc.dram_tensor("sgate", [FFI, D], BF16, kind="Internal").ap()
    sup = nc.dram_tensor("sup", [FFI, D], BF16, kind="Internal").ap()
    sdown = nc.dram_tensor("sdown", [D, FFI], BF16, kind="Internal").ap()

    pbR0 = nc.alloc_psum_tensor("pbR0", [128, 512], F32)
    pbR1 = nc.alloc_psum_tensor("pbR1", [128, 512], F32)
    pbO = nc.alloc_psum_tensor("pbO", [128, 512], F32)
    pbPT = nc.alloc_psum_tensor("pbPT", [128, 1024], F32)
    pbKH = nc.alloc_psum_tensor("pbKH", [128, 1024], BF16)
    pbST = nc.alloc_psum_tensor("pbST", [128, 1024], F32)

    A_ = nc.alloc_sbuf_tensor
    mask = A_("mask", [128, CH], F32)
    blkR = A_("s_blkR16", [128, 128], BF16)
    blk2 = A_("s_sel16", [16, 1024], BF16)
    ones1 = A_("s_ones1", [128, 1], BF16)
    onesb = A_("s_onesb", [1, 128], F32)
    zero64b = A_("s_zero64b", [128, CH], BF16)
    nw_col = A_("s_nw_col", [128, NFC], F32)
    c_col = A_("s_c_col", [128, NFC], F32)
    sc_col = A_("sc_col", [128, NFC], F32)
    modcol = A_("modcol", [128, 48], F32)
    colv = A_("colv", [128, 6 * NFC], F32)
    ACf32 = A_("ACf32", [128, NFC * 8], F32)
    tcol = A_("tcol", [128, 2 * NFC], F32)
    lnstA = A_("lnstA", [128, 2048], F32)
    mstat = lnstA[0:1, 0:512]
    istat_f = lnstA[0:1, 512:1024]
    vstat = lnstA[32:33, 0:512]
    lsc0 = lnstA[32:33, 512:1024]
    lsc1 = lnstA[32:33, 1024:1536]
    istat = lnstA[32:33, 1536:2048]
    rst = lnstA[64:80, 0:512]
    rs0 = lnstA[64:80, 512:1024]
    rs1 = lnstA[64:80, 1024:1536]
    rbf = A_("rbf", [16, 512], BF16)
    rbc = A_("rbc", [128, 512], BF16)
    S_sb = A_("S_sb", [128, NFC, HD], F32)

    ARENA = 196512
    arena = A_("arena", [128, ARENA], mybir.dt.uint8)
    arena_addr = nc.lookup_mloc(arena).addr
    cur = [0]

    arena_off = {}

    def aalloc(name, shape, dtype, at=None):
        per = 1
        for s in shape[1:]:
            per *= s
        per *= mybir.dt.size(dtype)
        if at is not None:
            off = at
        else:
            off = (cur[0] + 31) // 32 * 32
            cur[0] = off + per
            assert cur[0] <= ARENA, f"arena overflow at {name}: {cur[0]}"
        h = nc.alloc_sbuf_tensor_at(name, shape, dtype, offset=arena_addr + off)
        arena_off[name] = off
        return h

    wq = aalloc("wq", [128, NFC, D], BF16)
    wk = aalloc("wk", [128, NFC, D], BF16)
    wv = aalloc("wv", [128, NFC, D], BF16)
    wo = aalloc("wo", [128, NFC, D], BF16)
    wf1 = aalloc("wf1", [128, NFC, HD], BF16)
    wf2 = aalloc("wf2", [HD, NFC, 128], BF16)
    wg1 = aalloc("wg1", [128, NFC, HD], BF16)
    wg2 = aalloc("wg2", [HD, NFC, 128], BF16)
    xt = [aalloc(f"xt{i}", [128, NFC, 512], F32) for i in range(2)]
    h_sb = aalloc("h_sb", [128, NFC, 512], BF16)
    xbs = [aalloc(f"xbs{i}", [128, 512], BF16) for i in range(2)]
    xqs = [aalloc(f"xqs{i}", [128, 512], BF16) for i in range(2)]
    t0s = aalloc("t0s", [128, 512], F32)
    t1s = aalloc("t1s", [128, 512], F32)
    tanhs = [aalloc(f"tanhs{i}", [128, 512], BF16) for i in range(2)]
    u1s = [aalloc(f"u1s{i}", [128, 512], BF16) for i in range(2)]
    zvs = [aalloc(f"zvs{i}", [128, 512], BF16) for i in range(2)]
    fs = [aalloc(f"fs{i}", [128, 512], BF16) for i in range(2)]
    u1f_sb = aalloc("u1f_sb", [HD, 512], BF16)
    u1g_sb = aalloc("u1g_sb", [HD, 512], BF16)
    g1s = [aalloc(f"g1s{i}", [128, 512], BF16) for i in range(2)]
    invA = aalloc("invA", [128, NFC, 512], BF16)
    A_sb = aalloc("A_sb", [128, NFC, 512], BF16)
    qT = aalloc("qT", [128, NFC, 512], BF16)
    kT = aalloc("kT", [128, NFC, 512], BF16)
    tg_sb = aalloc("tg_sb", [128, NFC, 512], BF16)
    v_tok = aalloc("v_tok", [128, 4, D], BF16)
    og = aalloc("og_al", [128, NFC, 512], BF16, at=arena_off["invA"])
    ACbf = aalloc("ACbf", [128, NFC, NGC], BF16)
    Sbf = aalloc("ln.t0", [128, NFC, HD], BF16, at=arena_off["t0s"] + 1024)
    khtok_tile = aalloc("khtok_tile", [128, 4, D], BF16, at=arena_off["A_sb"])
    ptm = aalloc("ptm", [128, 2, NFC, CH], BF16, at=arena_off["xqs0"])
    sqs = aalloc("sqs", [128, NFC, CH], BF16, at=arena_off["t0s"])
    obf = aalloc("ln.t1", [128, NFC, CH], BF16, at=arena_off["t1s"])
    khT_pair = aalloc("khT_pair", [128, NFC, 128], BF16, at=arena_off["xbs0"])
    phaseA_end = cur[0]

    phaseA_full = cur[0]
    assert phaseA_full <= ARENA, phaseA_full

    cur[0] = 0
    wgT = aalloc("wgT", [128, NFC, FFI], BF16)
    wuT = aalloc("wuT", [128, NFC, FFI], BF16)
    wdT = aalloc("wdT", [128, NIC, D], BF16)
    xm = aalloc("xm", [128, NFC, 256], F32)
    xm2 = aalloc("xm2", [128, NFC, 256], F32, at=184224)
    h2_sb = aalloc("h2_sb", [128, 2, NFC, 128], BF16)
    h2tok = aalloc("h2tok", [128, D], BF16)
    h2qT = aalloc("h2qT", [128, 2, NFC, 128], BF16)
    hmqT = aalloc("hmqT", [128, 2, NIC, 128], BF16)
    xbs_b = [aalloc(f"xbsb{i}", [128, 256], BF16) for i in range(2)]
    xqs_b = [aalloc(f"xqsb{i}", [128, 256], BF16) for i in range(2)]
    t0b = aalloc("t0b", [128, 256], F32)
    t1b = aalloc("t1b", [128, 256], F32)
    tanb = aalloc("tanb", [128, 512], BF16)
    upbf = aalloc("upbf", [128, 512], BF16)
    gtbf = aalloc("gtbf", [128, 512], BF16)
    w1b = [aalloc(f"w1b{i}", [128, 512], BF16) for i in range(2)]
    QSC_B = []
    for bp in range(2):
        QSC_B.append({
            "a": aalloc(f"b{bp}_a", [128, 512], BF16),
            "t": aalloc(f"b{bp}_t", [128, 512], BF16),
            "v2": aalloc(f"b{bp}_v2", [128, 512], BF16),
            "sg": aalloc(f"b{bp}_sg", [128, 512], BF16),
            "bm": aalloc(f"b{bp}_bm", [128, 512 // 16], F32),
            "sc2": aalloc(f"b{bp}_sc2", [128, 512 // 16], F32),
            "sc2b": aalloc(f"b{bp}_sc2b", [128, 512 // 16], BF16),
        })
    phaseB_full = cur[0]
    assert phaseB_full <= ARENA, phaseB_full

    QSC_P, KEYMAP_P, WPIN_P, WPOUT_P = [], [], [], []
    for kpar in range(2):
        host = "qT" if kpar == 0 else "kT"
        base = arena_off[host]
        qsc = {
            "a": aalloc(f"qp{kpar}_a", [128, 1024], BF16, at=base),
            "t": aalloc(f"qp{kpar}_t", [128, 1024], BF16, at=base + 2048),
            "v2": aalloc(f"qp{kpar}_v2", [128, 1024], BF16, at=base + 4096),
            "sg": aalloc(f"qp{kpar}_sg", [128, 1024], BF16, at=base + 6144),
            "bm": aalloc(f"qp{kpar}_bm", [128, 64], F32,
                         at=182944 + kpar * 640),
            "sc2": aalloc(f"qp{kpar}_sc2", [128, 64], F32,
                          at=182944 + kpar * 640 + 256),
            "sc2b": aalloc(f"qp{kpar}_sc2b", [128, 64], BF16,
                           at=182944 + kpar * 640 + 512),
        }
        QSC_P.append(qsc)
        KEYMAP_P.append({"a": host, "t": host, "v2": host, "sg": host,
                         "bm": f"pst{kpar}", "sc": f"pst{kpar}",
                         "scb": f"pst{kpar}"})
        WPIN_P.append(aalloc(f"wpinp{kpar}", [128, 1024], BF16,
                             at=192416 + kpar * 2048))
        WPOUT_P.append(aalloc(f"wpoutp{kpar}", [128, 1024], BF16,
                              at=arena_off["tg_sb"] + kpar * 2048))

    p = Prog()

    def _finish_early():
        p.barrier()
        for i in range(NT_B):
            dst = y_fm[:, i * 256:(i + 1) * 256].rearrange("(fc p) t -> p fc t", p=128)
            p.op("sp", lambda e, d=dst: e.dma_start(d, xm[:]), reads=["xm0"],
                 dma=True)
        p.barrier()
        p.emit(nc)
        return nc

    class OpSink:
        def __init__(self):
            self.ops = []

        def op(self, *a, **k):
            self.ops.append((a, k))

    def zip_emit(*sinks):
        idx = 0
        while True:
            left = False
            for s in sinks:
                if idx < len(s.ops):
                    a, k = s.ops[idx]
                    p.op(*a, **k)
                    left = True
            if not left:
                break
            idx += 1

    def ladder_quant(eng2, src_key, src_ap, out_key, out_ap, width, qsc, kpfx,
                     key_map=None, eng3="dve", sink=None):
        q = sink if sink is not None else p
        nblk = width // 16
        a, t, v2, sg = qsc["a"], qsc["t"], qsc["v2"], qsc["sg"]
        bm, sc2, sc2b = qsc["bm"], qsc["sc2"], qsc["sc2b"]
        km = key_map or {}
        ka = km.get("a", kpfx + "a")
        kt = km.get("t", kpfx + "t")
        kv2 = km.get("v2", kpfx + "v2")
        ksg = km.get("sg", kpfx + "sg")
        kbm = km.get("bm", kpfx + "bm")
        ksc = km.get("sc", kpfx + "sc")
        kscb = km.get("scb", kpfx + "scb")
        src3 = src_ap.rearrange("p (n b) -> p n b", b=16)
        q.op("dve", lambda e: e.reduce_max(
            bm[:, 0:nblk].rearrange("p n -> p n ()"), src3, axis=AX.X,
            apply_absolute_value=True), reads=[src_key], writes=[kbm])
        q.op("dve", lambda e: e.tensor_scalar(
            sc2[:, 0:nblk], bm[:, 0:nblk], 1.0 / 12.0, 5e-13, ALU.mult, ALU.max),
            reads=[kbm], writes=[ksc])
        q.op("dve", lambda e: e.reciprocal(bm[:, 0:nblk], sc2[:, 0:nblk]),
             reads=[ksc], writes=[kbm])
        q.op("dve", lambda e: e.tensor_scalar(
            sc2b[:, 0:nblk], bm[:, 0:nblk], 0.5, None, ALU.mult),
            reads=[kbm], writes=[kscb])
        q.op(eng3, lambda e: e.tensor_tensor(
            a[:, 0:width].rearrange("p (n b) -> p n b", b=16), src3,
            sc2b[:, 0:nblk].rearrange("p n -> p n ()").broadcast_to([128, nblk, 16]),
            ALU.mult), reads=[src_key, kscb], writes=[ka])
        q.op("act", lambda e: e.activation(sg[:, 0:width], a[:, 0:width], AF.Sign),
             reads=[ka], writes=[ksg])
        q.op(eng2, lambda e: e.tensor_scalar(
            v2[:, 0:width], a[:, 0:width], -1.0, None, ALU.mult),
            reads=[ka], writes=[kv2])
        q.op(eng2, lambda e: e.tensor_tensor(
            a[:, 0:width], a[:, 0:width], v2[:, 0:width], ALU.max),
            reads=[ka, kv2], writes=[ka])
        q.op(eng2, lambda e: e.tensor_scalar(
            t[:, 0:width], a[:, 0:width], 0.25, None, ALU.is_gt),
            reads=[ka], writes=[kt])
        for mid in (0.75, 1.25, 1.75):
            q.op(eng2, lambda e, m=mid: e.scalar_tensor_tensor(
                t[:, 0:width], a[:, 0:width], m, t[:, 0:width], ALU.is_gt, ALU.add),
                reads=[ka, kt], writes=[kt])
        q.op(eng2, lambda e: e.tensor_scalar(
            v2[:, 0:width], a[:, 0:width], 2.5, None, ALU.is_gt),
            reads=[ka], writes=[kv2])
        q.op(eng2, lambda e: e.scalar_tensor_tensor(
            v2[:, 0:width], a[:, 0:width], 3.5, v2[:, 0:width], ALU.is_gt, ALU.add),
            reads=[ka, kv2], writes=[kv2])
        q.op(eng2, lambda e: e.scalar_tensor_tensor(
            t[:, 0:width], v2[:, 0:width], 2.0, t[:, 0:width], ALU.mult, ALU.add),
            reads=[kv2, kt], writes=[kt])
        q.op(eng2, lambda e: e.tensor_scalar(
            v2[:, 0:width], a[:, 0:width], 5.0, None, ALU.is_gt),
            reads=[ka], writes=[kv2])
        q.op(eng2, lambda e: e.scalar_tensor_tensor(
            t[:, 0:width], v2[:, 0:width], 4.0, t[:, 0:width], ALU.mult, ALU.add),
            reads=[kv2, kt], writes=[kt])
        q.op(eng3, lambda e: e.tensor_tensor(
            t[:, 0:width], t[:, 0:width], sg[:, 0:width], ALU.mult),
            reads=[kt, ksg], writes=[kt])
        q.op("dve", lambda e: e.tensor_copy(sc2b[:, 0:nblk], sc2[:, 0:nblk]),
             reads=[ksc], writes=[kscb])
        q.op(eng3, lambda e: e.tensor_tensor(
            out_ap.rearrange("p (n b) -> p n b", b=16),
            t[:, 0:width].rearrange("p (n b) -> p n b", b=16),
            sc2b[:, 0:nblk].rearrange("p n -> p n ()").broadcast_to([128, nblk, 16]),
            ALU.mult), reads=[kt, kscb], writes=[out_key])

    def rsqrt_via_sqrt(pfx, dst_ap, src_ap, scrA, scrB, rows, cols):
        p.op("act", lambda e: e.activation(scrA, src_ap, AF.Sqrt),
             reads=[pfx + "v"], writes=[pfx + "s0"])
        p.op("dve", lambda e: e.reciprocal(dst_ap, scrA),
             reads=[pfx + "s0"], writes=[pfx + "r"])
        p.op("dve", lambda e: e.tensor_mul(scrB, dst_ap, dst_ap),
             reads=[pfx + "r"], writes=[pfx + "t"])
        p.op("dve", lambda e: e.tensor_mul(scrB, scrB, src_ap),
             reads=[pfx + "t", pfx + "v"], writes=[pfx + "t"])
        p.op("dve", lambda e: e.tensor_scalar(scrB, scrB, -0.5, 1.5, ALU.mult, ALU.add),
             reads=[pfx + "t"], writes=[pfx + "t"])
        p.op("dve", lambda e: e.tensor_mul(dst_ap, dst_ap, scrB),
             reads=[pfx + "r", pfx + "t"], writes=[pfx + "r"])

    def layernorm(xbuf, xkey, T, scol_off, hout, hkey, xbp, xqp, t0x, t1x,
                  tin=None):
        for fc in range(NFC):
            pa = fc % 2
            p.op("dve", lambda e, fc=fc, pa=pa: e.tensor_copy(
                xbp[pa][:, 0:T], xbuf[:, fc, 0:T]), reads=[xkey], writes=[f"xb{pa}"])
            p.op("act", lambda e, pa=pa: e.activation(
                xqp[pa][:, 0:T], xbp[pa][:, 0:T], AF.Square),
                reads=[f"xb{pa}"], writes=[f"xq{pa}"])
            p.op("pe", lambda e, pa=pa, fc=fc: e.matmul(
                pbST[0:1, 0:T], ones1[:], xbp[pa][:, 0:T],
                start=(fc == 0), stop=(fc == NFC - 1)),
                reads=[f"xb{pa}", "ones1"], wclaim=(("ST",) if fc == 0 else ()),
                writes=[], inc=True)
            p.op("pe", lambda e, pa=pa, fc=fc: e.matmul(
                pbST[0:1, 512:512 + T], ones1[:], xqp[pa][:, 0:T],
                start=(fc == 0), stop=(fc == NFC - 1)),
                reads=[f"xq{pa}", "ones1"], wclaim=(("ST",) if fc == 0 else ()),
                writes=(["ST"] if fc == NFC - 1 else []), inc=True)
        p.op("dve", lambda e: e.tensor_scalar(
            mstat[0:1, 0:T], pbST[0:1, 0:T], 1.0 / D, None, ALU.mult),
            reads=["ST"], writes=["ln.m"])
        p.op("dve", lambda e: e.tensor_scalar(
            vstat[0:1, 0:T], pbST[0:1, 512:512 + T], 1.0 / D, EPS, ALU.mult, ALU.add),
            reads=["ST"], writes=["ln.v"])
        p.op("dve", lambda e: e.tensor_mul(lsc0[0:1, 0:T], mstat[0:1, 0:T],
                                           mstat[0:1, 0:T]),
             reads=["ln.m"], writes=["ln.m2"])
        p.op("dve", lambda e: e.tensor_sub(vstat[0:1, 0:T], vstat[0:1, 0:T],
                                           lsc0[0:1, 0:T]),
             reads=["ln.v", "ln.m2"], writes=["ln.v"])
        rsqrt_via_sqrt("ln.", istat[0:1, 0:T], vstat[0:1, 0:T],
                       lsc0[0:1, 0:T], lsc1[0:1, 0:T], 1, T)
        p.op("dve", lambda e: e.tensor_copy(istat_f[0:1, 0:T], istat[0:1, 0:T]),
             reads=["ln.r"], writes=["ln.bc2"])
        p.op("pe", lambda e: e.matmul(pbR0[:, 0:T], onesb[:], mstat[0:1, 0:T],
                                      start=True, stop=True),
             reads=["ln.m", "onesb"], writes=["R0"])
        p.op("pe", lambda e: e.matmul(pbR1[:, 0:T], onesb[:], istat_f[0:1, 0:T],
                                      start=True, stop=True),
             reads=["ln.bc2", "onesb"], writes=["R1"])
        for fc in range(NFC):
            tx, tk = (t0x, "ln.t0") if fc % 2 == 0 else (t1x, "ln.t1")
            p.op("dve", lambda e, fc=fc, tx=tx: e.tensor_sub(
                tx[:, 0:T], xbuf[:, fc, 0:T], pbR0[:, 0:T]),
                reads=[xkey, "R0"], writes=[tk])
            p.op("dve", lambda e, tx=tx: e.tensor_mul(
                tx[:, 0:T], tx[:, 0:T], pbR1[:, 0:T]),
                reads=[tk, "R1"], writes=[tk])
            p.op("dve", lambda e, fc=fc, tx=tx: e.tensor_scalar(
                hout(fc), (tin(tx) if tin else tx[:, 0:T]),
                colv[:, scol_off + 8 + fc:scol_off + 8 + fc + 1],
                colv[:, scol_off + fc:scol_off + fc + 1],
                ALU.mult, ALU.add), reads=[tk, "colv"], writes=[hkey])

    for dst, src, key in [(mask, mask_d, "mask"),
                          (blkR, blkR_d, "blkR"), (blk2, blk2_d, "blk2"),
                          (ones1, ones1_d, "ones1"), (onesb, onesb_d, "onesb"),
                          (zero64b, zero64b_d, "zero"), (nw_col, nw_col_d, "nw"),
                          (c_col, c_col_d, "c_col")]:
        p.op("sp", lambda e, d=dst, s=src: e.dma_start(d[:], s), writes=[key], dma=True)
    p.op("sp", lambda e: e.dma_start(S_sb[:].rearrange("p a b -> p (a b)"),
                                     zero512_d), writes=["S"], dma=True)
    for dst, src, key in [(wf1, wf1_d, "wf1"), (wf2, wf2_d, "wf2"),
                          (wq, w_d["Wq"], "wq"), (wk, w_d["Wk"], "wk"),
                          (wv, w_d["Wv"], "wv"), (wg1, wg1_d, "wg1"),
                          (wg2, wg2_d, "wg2"), (wo, w_d["Wo"], "wo")]:
        if key in ("wf2", "wg2"):
            rs = src.rearrange("k (fc f) -> k fc f", fc=NFC)
        else:
            rs = src.rearrange("(kc p) f -> p kc f", p=128)
        p.op("pool", lambda e, d=dst, s=rs: e.dma_start(d[:], s), writes=[key], dma=True)

    if stop == "consts":
        return _finish_early()
    p.op("act", lambda e: e.activation(tcol[:, 0:NFC], c_col[:], AF.Tanh, scale=0.5),
         reads=["c_col"], writes=["tanc"])
    p.op("dve", lambda e: e.scalar_tensor_tensor(
        tcol[:, 0:NFC], tcol[:, 0:NFC], 1.0, c_col[:], ALU.add, ALU.mult),
        reads=["tanc", "c_col"], writes=["tanc"])
    p.op("dve", lambda e: e.tensor_scalar(sc_col[:], tcol[:, 0:NFC], 0.5, None,
                                          ALU.mult), reads=["tanc"], writes=["sc_col"])

    wmst = [aalloc(f"wmst{i}", [128, NFC, 512], F32, at=arena_off[f"xt{i}"])
            for i in range(2)]
    for foc in range(12):
        pa = foc % 2
        src = wmod_d[:, foc * 512:(foc + 1) * 512].rearrange("(kc p) f -> p kc f", p=128)
        p.op("sp" if pa == 0 else "act", lambda e, s=src, pa=pa: e.dma_start(
            wmst[pa][:], s), writes=[f"xt{pa}"], dma=True)
        for sub in range(4):
            col = foc * 4 + sub
            for kc in range(NFC):
                p.op("pe", lambda e, pa=pa, sub=sub, kc=kc, col=col: e.matmul(
                    pbPT[:, col:col + 1], wmst[pa][:, kc, sub * 128:(sub + 1) * 128],
                    sc_col[:, kc:kc + 1], start=(kc == 0), stop=(kc == NFC - 1)),
                    reads=[f"xt{pa}", "sc_col"], wclaim=(("PT",) if kc == 0 else ()),
                    writes=(["PT"] if kc == NFC - 1 else []), inc=(kc == NFC - 1))
    p.op("dve", lambda e: e.tensor_copy(modcol[:], pbPT[:, 0:48]),
         reads=["PT"], writes=["modcol"])
    p.op("dve", lambda e: e.tensor_copy(colv[:, 0:8], modcol[:, 0:8]),
         reads=["modcol"], writes=["colv"])
    p.op("dve", lambda e: e.tensor_scalar(colv[:, 8:16], modcol[:, 8:16], 1.0, None,
                                          ALU.add), reads=["modcol"], writes=["colv"])
    p.op("dve", lambda e: e.tensor_copy(colv[:, 24:32], modcol[:, 24:32]),
         reads=["modcol"], writes=["colv"])
    p.op("dve", lambda e: e.tensor_scalar(colv[:, 32:40], modcol[:, 32:40], 1.0, None,
                                          ALU.add), reads=["modcol"], writes=["colv"])
    for co, ci in ((16, 16), (40, 40)):
        p.op("dve", lambda e, ci=ci: e.tensor_scalar(
            tcol[:, 8:16], modcol[:, ci:ci + 8], -1.0, None, ALU.mult),
            reads=["modcol"], writes=["gneg"])
        p.op("dve", lambda e, co=co, ci=ci: e.tensor_tensor(
            colv[:, co:co + 8], modcol[:, ci:ci + 8], tcol[:, 8:16], ALU.max),
            reads=["modcol", "gneg"], writes=["colv"])
        p.op("dve", lambda e, co=co: e.tensor_scalar(
            colv[:, co:co + 8], colv[:, co:co + 8], 0.25, None, ALU.max),
            reads=["colv"], writes=["colv"])
        p.op("dve", lambda e, ci=ci: e.tensor_scalar(
            tcol[:, 0:8], modcol[:, ci:ci + 8], 0.0, 2.0, ALU.is_ge, ALU.mult),
            reads=["modcol"], writes=["gsn"])
        p.op("dve", lambda e: e.tensor_scalar(tcol[:, 0:8], tcol[:, 0:8], -1.0, None,
                                              ALU.add), reads=["gsn"], writes=["gsn"])
        p.op("dve", lambda e, co=co: e.tensor_mul(colv[:, co:co + 8],
                                                  colv[:, co:co + 8], tcol[:, 0:8]),
             reads=["colv", "gsn"], writes=["colv"])

    if stop == "mod":
        return _finish_early()
    prep_tasks = []
    prep_units = []

    def _mk_prep(src_d, dst_d, nrows, K):
        pieces = [(0, 1024), (1024, 1024), (2048, 768)] if K == FFI else [(0, K)]
        for r in range(nrows):
            for (k0, kw) in pieces:
                prep_units.append((src_d, dst_d, r, k0, kw))
    _mk_prep(wgate_d, sgate, NIC, D)
    _mk_prep(wup_d, sup, NIC, D)
    _mk_prep(wdown_d, sdown, NFC, FFI)

    for _ui in range(0, len(prep_units), 2):
        pair = [(kp, prep_units[_ui + kp]) for kp in range(2)
                if _ui + kp < len(prep_units)]

        def task(pair=pair):
            for kp, (src_d, dst_d, r, k0, kw) in pair:
                srcap = src_d[r * 128:(r + 1) * 128, k0:k0 + kw]
                p.op("pool", lambda e, s=srcap, kp=kp, kw=kw: e.dma_start(
                    WPIN_P[kp][:, 0:kw], s), writes=[f"wpin{kp}"], dma=True)
            sks = []
            for kp, (src_d, dst_d, r, k0, kw) in pair:
                s = OpSink()
                ladder_quant("dve", f"wpin{kp}", WPIN_P[kp][:, 0:kw], "tg",
                             WPOUT_P[kp][:, 0:kw], kw, QSC_P[kp], "qw.",
                             key_map=KEYMAP_P[kp], sink=s)
                sks.append(s)
            zip_emit(*sks)
            for kp, (src_d, dst_d, r, k0, kw) in pair:
                dst = dst_d[r * 128:(r + 1) * 128, k0:k0 + kw]
                p.op("act", lambda e, d=dst, kp=kp, kw=kw: e.dma_start(
                    d, WPOUT_P[kp][:, 0:kw]), reads=["tg"], dma=True)
        prep_tasks.append(task)

    if stop == "prep":
        for t in prep_tasks:
            t()
        prep_tasks = []
        return _finish_early()

    def load_xtile(i):
        pa = i % 2
        src = xT[:, i * 512:(i + 1) * 512].rearrange("(fc p) t -> p fc t", p=128)
        p.op("sp", lambda e, s=src, pa=pa: e.dma_start(xt[pa][:], s),
             writes=[f"xt{pa}"], dma=True)

    def lnA(i):
        layernorm(xt[i % 2][:], f"xt{i % 2}", 512, 0,
                  lambda fc: h_sb[:, fc, 0:512], "h", xbs, xqs, t0s, t1s)

    load_xtile(0)
    lnA(0)
    for i in range(NT_A):
        pa_t = i % 2
        if i + 1 < NT_A:
            load_xtile(i + 1)
        xk = f"xt{pa_t}"
        xb = xt[pa_t]

        if stop == "A_ln":
            return _finish_early()

        for kc in range(NFC):
            p.op("pe", lambda e, kc=kc: e.matmul(
                pbR0[0:HD, 0:512], wf1[:, kc, :], h_sb[:, kc, :],
                start=(kc == 0), stop=(kc == NFC - 1)),
                reads=["h", "wf1"], wclaim=(("R0",) if kc == 0 else ()),
                writes=(["R0"] if kc == NFC - 1 else []), inc=(kc == NFC - 1))
        p.op("dve", lambda e: e.tensor_copy(u1f_sb[:, :], pbR0[0:HD, 0:512]),
             reads=["R0"], writes=["u1f"])
        for foc in range(NFC):
            pa = foc % 2
            bank, bk = (pbR0, "R0") if pa == 0 else (pbR1, "R1")
            p.op("pe", lambda e, foc=foc, b=bank: e.matmul(
                b[:, 0:512], wf2[:, foc, :], u1f_sb[:, :], start=True, stop=True),
                reads=["u1f", "wf2"], writes=[bk])
            p.op("act", lambda e, b=bank, pa=pa: e.activation(
                g1s[pa][:, :], b[:, 0:512], AF.Exp, scale=-1.0),
                reads=[bk], writes=[f"g1{pa}"])
            p.op("act", lambda e, b=bank, pa=pa: e.activation(
                fs[pa][:, :], b[:, 0:512], AF.Tanh, scale=0.5),
                reads=[bk], writes=[f"fs{pa}"])
            p.op("dve", lambda e, pa=pa: e.tensor_scalar(
                g1s[pa][:, :], g1s[pa][:, :], 1.0, None, ALU.add),
                reads=[f"g1{pa}"], writes=[f"g1{pa}"])
            p.op("dve", lambda e, pa=pa: e.tensor_scalar(
                fs[pa][:, :], fs[pa][:, :], 0.5, 0.5, ALU.mult, ALU.add),
                reads=[f"fs{pa}"], writes=[f"fs{pa}"])
            for gc in range(NGC):
                p.op("dve", lambda e, foc=foc, gc=gc, pa=pa: e.tensor_tensor_scan(
                    invA[:, foc, gc * CH:(gc + 1) * CH],
                    g1s[pa][:, gc * CH:(gc + 1) * CH], zero64b[:], 1.0,
                    ALU.mult, ALU.add),
                    reads=[f"g1{pa}", "zero"], writes=["invA"])
                p.op("dve", lambda e, foc=foc, gc=gc, pa=pa: e.tensor_tensor_scan(
                    A_sb[:, foc, gc * CH:(gc + 1) * CH],
                    fs[pa][:, gc * CH:(gc + 1) * CH], zero64b[:], 1.0,
                    ALU.mult, ALU.add),
                    reads=[f"fs{pa}", "zero"], writes=["Agrp"])
        p.op("dve", lambda e: e.tensor_copy(
            ACbf[:], A_sb[:].rearrange("p a (g c) -> p a g c", c=CH)[:, :, :, CH - 1]),
            reads=["Agrp"], writes=["AC"])
        p.op("dve", lambda e: e.tensor_copy(
            ACf32[:].rearrange("p (a g) -> p a g", g=NGC),
            A_sb[:].rearrange("p a (g c) -> p a g c", c=CH)[:, :, :, CH - 1]),
            reads=["Agrp"], writes=["ACf"])

        if stop == "A_f":
            return _finish_early()
        for (wmat, wkey, sc_buf, sckey, dbuf, dkey, zsc) in (
                (wq, "wq", A_sb, "Agrp", qT, "qT", 0.5 * SCALE),
                (wk, "wk", invA, "invA", kT, "kT", 0.5)):
            for foc in range(NFC):
                pa = foc % 2
                bank, bk = (pbR0, "R0") if pa == 0 else (pbR1, "R1")
                for kc in range(NFC):
                    p.op("pe", lambda e, foc=foc, kc=kc, b=bank, w=wmat: e.matmul(
                        b[:, 0:512], w[:, kc, foc * 128:(foc + 1) * 128],
                        h_sb[:, kc, :], start=(kc == 0), stop=(kc == NFC - 1)),
                        reads=["h", wkey], wclaim=((bk,) if kc == 0 else ()),
                        writes=([bk] if kc == NFC - 1 else []), inc=(kc == NFC - 1))
                p.op("act", lambda e, b=bank, pa=pa: e.activation(
                    tanhs[pa][:, :], b[:, 0:512], AF.Tanh, scale=0.5),
                    reads=[bk], writes=[f"tanh{pa}"])
                p.op("act", lambda e, b=bank, z=zsc, pa=pa: e.activation(
                    zvs[pa][:, :], b[:, 0:512], AF.Copy, scale=z),
                    reads=[bk], writes=[f"zv{pa}"])
                p.op("dve", lambda e, foc=foc, sb_=sc_buf, pa=pa:
                     e.scalar_tensor_tensor(
                    u1s[pa][:, :], tanhs[pa][:, :], 1.0, sb_[:, foc, :],
                    ALU.add, ALU.mult),
                    reads=[f"tanh{pa}", sckey], writes=[f"u1{pa}"])
                p.op("dve", lambda e, foc=foc, d=dbuf, pa=pa: e.tensor_mul(
                    d[:, foc, :], zvs[pa][:, :], u1s[pa][:, :]),
                    reads=[f"zv{pa}", f"u1{pa}"], writes=[dkey])

        if stop == "A_qk":
            return _finish_early()
        for sub in range(4):
            for half in range(2):
                bank, bk = (pbPT, "PT") if half == 0 else (pbO, "O")
                for kc in range(NFC):
                    p.op("pe", lambda e, sub=sub, half=half, kc=kc, b=bank: e.matmul(
                        b[:, 0:512], h_sb[:, kc, sub * 128:(sub + 1) * 128],
                        wv[:, kc, half * 512:(half + 1) * 512],
                        start=(kc == 0), stop=(kc == NFC - 1)),
                        reads=["h", "wv"], wclaim=((bk,) if kc == 0 else ()),
                        writes=([bk] if kc == NFC - 1 else []), inc=(kc == NFC - 1))
                p.op("dve", lambda e, sub=sub, half=half, b=bank: e.tensor_copy(
                    v_tok[:, sub, half * 512:(half + 1) * 512], b[:, 0:512]),
                    reads=[bk], writes=["v"])

        if stop == "A_v":
            return _finish_early()
        for kc in range(NFC):
            p.op("pe", lambda e, kc=kc: e.matmul(
                pbR0[0:HD, 0:512], wg1[:, kc, :], h_sb[:, kc, :],
                start=(kc == 0), stop=(kc == NFC - 1)),
                reads=["h", "wg1"], wclaim=(("R0",) if kc == 0 else ()),
                writes=(["R0"] if kc == NFC - 1 else []), inc=(kc == NFC - 1))
        p.op("dve", lambda e: e.tensor_copy(u1g_sb[:, :], pbR0[0:HD, 0:512]),
             reads=["R0"], writes=["u1g"])
        for foc in range(NFC):
            pa = foc % 2
            bank, bk = (pbR0, "R0") if pa == 0 else (pbR1, "R1")
            p.op("pe", lambda e, foc=foc, b=bank: e.matmul(
                b[:, 0:512], wg2[:, foc, :], u1g_sb[:, :], start=True, stop=True),
                reads=["u1g", "wg2"], writes=[bk])
            p.op("act", lambda e, foc=foc, b=bank: e.activation(
                tg_sb[:, foc, :], b[:, 0:512], AF.Tanh, scale=0.5),
                reads=[bk], writes=["tg"])

        if stop == "A_g":
            return _finish_early()
        if i + 1 < NT_A:
            lnA(i + 1)
        for pr in range(4):
            p.op("dve", lambda e, pr=pr: e.tensor_tensor(
                khT_pair[:].rearrange("p a (g c) -> p a g c", c=CH),
                kT[:, :, pr * 128:(pr + 1) * 128].rearrange(
                    "p a (g c) -> p a g c", c=CH),
                ACbf[:, :, 2 * pr:2 * pr + 2].rearrange("p a g -> p a g ()")
                .broadcast_to([128, NFC, 2, CH]), ALU.mult),
                reads=["kT", "AC"], writes=["xb0", "xb1"])
            p.op("sp", lambda e, pr=pr: e.dma_start(
                khtok_tile[:, pr, :].rearrange("p (a b) -> p a b", b=128),
                khT_pair[:], transpose=True),
                reads=["xb0", "xb1"], writes=["Agrp"], dma=True)

        for gc in range(NGC):
            gpar = gc % 2
            sub = gc // 2
            sbase = gpar * 64
            tsl = slice(gc * CH, (gc + 1) * CH)
            p.op("act", lambda e: e.activation(
                Sbf[:].rearrange("p a c -> p (a c)"),
                S_sb[:].rearrange("p a c -> p (a c)"), AF.Copy),
                reads=["S"], writes=["ln.t0"])
            for hp in range(2):
                for fc in range(NFC):
                    p.op("pe", lambda e, hp=hp, fc=fc, sbase=sbase, tsl=tsl: e.matmul(
                        pbPT[sbase:sbase + 64, hp * 512 + fc * 64:hp * 512 + (fc + 1) * 64],
                        kT[hp * 64:(hp + 1) * 64, fc, tsl],
                        qT[hp * 64:(hp + 1) * 64, fc, tsl],
                        start=(fc == 0), stop=(fc == NFC - 1),
                        tile_position=(hp * 64, sbase)),
                        reads=["kT", "qT"],
                        wclaim=(("PT",) if (hp == 0 and fc == 0) else ()),
                        writes=(["PT"] if (hp == 1 and fc == NFC - 1) else []),
                        inc=(fc == NFC - 1))
            p.op("dve", lambda e, sbase=sbase: e.tensor_tensor(
                ptm[sbase:sbase + 64, :, :, :],
                pbPT[sbase:sbase + 64, :].rearrange("p (b f c) -> p b f c", b=2, c=CH),
                mask[sbase:sbase + 64, :].rearrange("p c -> p () () c").broadcast_to(
                    [64, 2, NFC, CH]), ALU.mult),
                reads=["PT", "mask"], writes=["xq0", "xq1"])
            for hp in range(2):
                for fc in range(NFC):
                    hh = 2 * fc + hp
                    p.op("pe", lambda e, hh=hh, hp=hp, fc=fc, sbase=sbase, sub=sub:
                         e.matmul(
                        pbO[hp * 64:(hp + 1) * 64, fc * 64:(fc + 1) * 64],
                        v_tok[sbase:sbase + 64, sub, hh * 64:(hh + 1) * 64],
                        ptm[sbase:sbase + 64, hp, fc, :],
                        start=(fc == 0), stop=(fc == NFC - 1),
                        tile_position=(sbase, hp * 64)),
                        reads=["v", "xq0", "xq1"],
                        wclaim=(("O",) if (hp == 0 and fc == 0) else ()),
                        writes=(["O"] if (hp == 1 and fc == NFC - 1) else []),
                        inc=(fc == NFC - 1))
            for hp in range(2):
                for fc in range(NFC):
                    p.op("pe", lambda e, hp=hp, fc=fc, tsl=tsl: e.matmul(
                        pbR0[hp * 64:(hp + 1) * 64, fc * 64:(fc + 1) * 64],
                        Sbf[hp * 64:(hp + 1) * 64, fc, :],
                        qT[hp * 64:(hp + 1) * 64, fc, tsl],
                        start=(fc == 0), stop=(fc == NFC - 1),
                        tile_position=(hp * 64, hp * 64)),
                        reads=["ln.t0", "qT"],
                        wclaim=(("R0",) if (hp == 0 and fc == 0) else ()),
                        writes=(["R0"] if (hp == 1 and fc == NFC - 1) else []),
                        inc=(fc == NFC - 1))
            p.op("act", lambda e: e.activation(
                obf[:].rearrange("p a c -> p (a c)"), pbO[:, 0:512], AF.Copy),
                reads=["O"], writes=["ln.t1"])
            p.op("dve", lambda e: e.tensor_tensor(
                obf[:].rearrange("p a c -> p (a c)"),
                obf[:].rearrange("p a c -> p (a c)"), pbR0[:, 0:512], ALU.add),
                reads=["ln.t1", "R0"], writes=["ln.t1"])
            p.op("dve", lambda e, tsl=tsl: e.scalar_tensor_tensor(
                og[:, :, tsl], tg_sb[:, :, tsl], 1.0, obf[:, :, :],
                ALU.add, ALU.mult), reads=["tg", "ln.t1"], writes=["invA"])
            p.op("pool", lambda e, tsl=tsl: e.tensor_tensor(
                sqs[:, :, :], og[:, :, tsl], og[:, :, tsl], ALU.mult),
                reads=["invA"], writes=["ln.t0"])
            for fc in range(NFC):
                p.op("pe", lambda e, fc=fc, gc=gc: e.matmul(
                    pbST[0:16, gc * 64:(gc + 1) * 64],
                    blkR[:, fc * 16:(fc + 1) * 16],
                    sqs[:, fc, :], start=(fc == 0), stop=(fc == NFC - 1)),
                    reads=["ln.t0", "blkR"], wclaim=(("ST",) if fc == 0 else ()),
                    writes=(["ST"] if fc == NFC - 1 else []), inc=(fc == NFC - 1))
            for hp in range(2):
                for fc in range(NFC):
                    hh = 2 * fc + hp
                    p.op("pe", lambda e, hh=hh, hp=hp, fc=fc, sbase=sbase, sub=sub:
                         e.matmul(
                        pbO[hp * 64:(hp + 1) * 64, fc * 64:(fc + 1) * 64],
                        khtok_tile[sbase:sbase + 64, sub, hh * 64:(hh + 1) * 64],
                        v_tok[sbase:sbase + 64, sub, hh * 64:(hh + 1) * 64],
                        start=(fc == 0), stop=(fc == NFC - 1),
                        tile_position=(sbase, hp * 64)),
                        reads=["Agrp", "v"],
                        wclaim=(("O",) if (hp == 0 and fc == 0) else ()),
                        writes=(["O"] if (hp == 1 and fc == NFC - 1) else []),
                        inc=(fc == NFC - 1))
            p.op("pool", lambda e, gc=gc: e.tensor_tensor(
                S_sb[:, :, :], S_sb[:, :, :],
                ACf32[:].rearrange("p (a g) -> p a g", g=NGC)[:, :, gc:gc + 1]
                .broadcast_to([128, NFC, HD]), ALU.mult),
                reads=["S", "ACf"], writes=["S"])
            p.op("dve", lambda e: e.tensor_tensor(
                S_sb[:].rearrange("p a c -> p (a c)"),
                S_sb[:].rearrange("p a c -> p (a c)"), pbO[:, 0:512], ALU.add),
                reads=["S", "O"], writes=["S"])

        p.op("dve", lambda e: e.tensor_scalar(
            rst[0:16, 0:512], pbST[0:16, 0:512], 1.0 / HD, 4 * EPS, ALU.mult, ALU.add),
            reads=["ST"], writes=["rms.v"])
        rsqrt_via_sqrt("rms.", rs1[0:16, 0:512], rst[0:16, 0:512], rs0[0:16, 0:512], rs0[0:16, 0:512], 16, 512)
        p.op("dve", lambda e: e.tensor_copy(rbf[:], rs1[0:16, 0:512]),
             reads=["rms.r"], writes=["rbf"])
        for fc in range(NFC):
            pa = fc % 2
            bank, bk = (pbR0, "R0") if pa == 0 else (pbR1, "R1")
            p.op("pe", lambda e, fc=fc, b=bank: e.matmul(
                b[:, 0:512], blk2[:, fc * 128:(fc + 1) * 128], rbf[0:16, :],
                start=True, stop=True),
                reads=["rbf", "blk2"], writes=[bk])
            p.op("act", lambda e, b=bank: e.activation(
                rbc[:, :], b[:, 0:512], AF.Copy), reads=[bk], writes=["rbc"])
            p.op("dve", lambda e, fc=fc: e.scalar_tensor_tensor(
                og[:, fc, :], og[:, fc, :], nw_col[:, fc:fc + 1], rbc[:, :],
                ALU.mult, ALU.mult), reads=["invA", "nw", "rbc"], writes=["invA"])
        for foc in range(NFC):
            pa = foc % 2
            bank, bk = (pbR0, "R0") if pa == 0 else (pbR1, "R1")
            for kc in range(NFC):
                p.op("pe", lambda e, foc=foc, kc=kc, b=bank: e.matmul(
                    b[:, 0:512], wo[:, kc, foc * 128:(foc + 1) * 128], og[:, kc, :],
                    start=(kc == 0), stop=(kc == NFC - 1)),
                    reads=["invA", "wo"], wclaim=((bk,) if kc == 0 else ()),
                    writes=([bk] if kc == NFC - 1 else []), inc=(kc == NFC - 1))
            p.op("dve", lambda e, foc=foc, b=bank, xb=xb: e.scalar_tensor_tensor(
                xb[:, foc, :], b[:, 0:512], colv[:, 16 + foc:16 + foc + 1],
                xb[:, foc, :], ALU.mult, ALU.add),
                reads=[bk, "colv", xk], writes=[xk])
        dst = xmid[:, i * 512:(i + 1) * 512].rearrange("(fc p) t -> p fc t", p=128)
        p.op("sp", lambda e, d=dst, xb=xb: e.dma_start(d, xb[:]), reads=[xk], dma=True)
        nslice = (len(prep_tasks) + NT_A - 1 - i) // max(1, NT_A - i)
        for _ in range(min(nslice, len(prep_tasks))):
            prep_tasks.pop(0)()

    while prep_tasks:
        prep_tasks.pop(0)()

    p.barrier()

    for kc in range(NFC):
        p.op("act", lambda e, kc=kc: e.dma_start(
            wgT[:, kc, :], sgate[:, kc * 128:(kc + 1) * 128], transpose=True),
            writes=["wgT"], dma=True)
        p.op("act", lambda e, kc=kc: e.dma_start(
            wuT[:, kc, :], sup[:, kc * 128:(kc + 1) * 128], transpose=True),
            writes=["wuT"], dma=True)
    for ic in range(NIC):
        p.op("act", lambda e, ic=ic: e.dma_start(
            wdT[:, ic, :], sdown[:, ic * 128:(ic + 1) * 128], transpose=True),
            writes=["wdT"], dma=True)

    nch_list = []
    n0 = 0
    while n0 < FFI:
        nch_list.append((n0, min(512, FFI - n0)))
        n0 += 512

    for bp in range(2):
        p.op("pool", lambda e, bp=bp: e.memset(w1b[bp][:], 0.0),
             reads=[], writes=[f"hm{bp}"])

    xmbs = [xm, xm2]

    def lnb(i):
        xmb = xmbs[i % 2]
        xmk = f"xm{i % 2}"
        src = xmid[:, i * 256:(i + 1) * 256].rearrange("(fc p) t -> p fc t", p=128)
        p.op("sp", lambda e, s=src, xmb=xmb: e.dma_start(xmb[:], s),
             writes=[xmk], dma=True)
        layernorm(xmb[:], xmk, 256, 24,
                  lambda fc: h2_sb[:, :, fc, :],
                  "h2", xbs_b, xqs_b, t0b, t1b,
                  tin=lambda t: t[:, 0:256].rearrange("p (s b) -> p s b", b=128))

    lnb(0)
    for i in range(NT_B):
        xmb = xmbs[i % 2]
        xmk = f"xm{i % 2}"

        def h2_in(sub):
            ssl = slice(sub * 128, (sub + 1) * 128)
            p.op("sp", lambda e, sub=sub: e.dma_start(
                h2tok[:].rearrange("p (a b) -> p a b", b=128),
                h2_sb[:, sub, :, :], transpose=True),
                reads=["h2"], writes=["h2tok"], dma=True)
        def h2_lad():
            _sk = []
            for half in range(2):
                hsl = slice(half * 512, (half + 1) * 512)
                s = OpSink()
                ladder_quant("dve", "h2tok", h2tok[:, hsl], "h2tok",
                             h2tok[:, hsl], 512, QSC_B[half], f"qb{half}.",
                             sink=s)
                _sk.append(s)
            zip_emit(*_sk)
        def h2_out(sub):
            p.op("act", lambda e, sub=sub: e.dma_start(
                h2qT[:, sub, :, :],
                h2tok[:].rearrange("p (a b) -> p a b", b=128), transpose=True),
                writes=["h2tok", "h2qT"], dma=True)
        def nch_block(sub):
            for pj in range(0, len(nch_list), 2):
                pcs = []
                for bp in range(2):
                    nn0, nwd = nch_list[pj + bp]
                    gb_, gk = (pbR0, "R0") if bp == 0 else (pbO, "O")
                    ub_, uk = (pbR1, "R1") if bp == 0 else (pbPT, "PT")
                    pcs.append((bp, nn0, nwd, gb_, gk, ub_, uk))
                for bp, nn0, nwd, gb_, gk, ub_, uk in pcs:
                    for kc in range(NFC):
                        p.op("pe", lambda e, kc=kc, nn0=nn0, nwd=nwd, sub=sub,
                             gb_=gb_: e.matmul(
                            gb_[:, 0:nwd], h2qT[:, sub, kc, :],
                            wgT[:, kc, nn0:nn0 + nwd],
                            start=(kc == 0), stop=(kc == NFC - 1)),
                            reads=["h2qT", "wgT"],
                            wclaim=((gk,) if kc == 0 else ()),
                            writes=([gk] if kc == NFC - 1 else []),
                            inc=(kc == NFC - 1))
                    for kc in range(NFC):
                        p.op("pe", lambda e, kc=kc, nn0=nn0, nwd=nwd, sub=sub,
                             ub_=ub_: e.matmul(
                            ub_[:, 0:nwd], h2qT[:, sub, kc, :],
                            wuT[:, kc, nn0:nn0 + nwd],
                            start=(kc == 0), stop=(kc == NFC - 1)),
                            reads=["h2qT", "wuT"],
                            wclaim=((uk,) if kc == 0 else ()),
                            writes=([uk] if kc == NFC - 1 else []),
                            inc=(kc == NFC - 1))
                for bp, nn0, nwd, gb_, gk, ub_, uk in pcs:
                    p.op("act", lambda e, nwd=nwd, gb_=gb_: e.activation(
                        tanb[:, 0:nwd], gb_[:, 0:nwd], AF.Tanh, scale=0.5),
                        reads=[gk], writes=["tanb"])
                    p.op("act", lambda e, nwd=nwd, gb_=gb_: e.activation(
                        gtbf[:, 0:nwd], gb_[:, 0:nwd], AF.Copy, scale=0.5),
                        reads=[gk], writes=["gtbf"])
                    p.op("act", lambda e, nwd=nwd, ub_=ub_: e.activation(
                        upbf[:, 0:nwd], ub_[:, 0:nwd], AF.Copy),
                        reads=[uk], writes=["upbf"])
                    p.op("dve", lambda e, nwd=nwd, bp=bp: e.scalar_tensor_tensor(
                        w1b[bp][:, 0:nwd], tanb[:, 0:nwd], 1.0, upbf[:, 0:nwd],
                        ALU.add, ALU.mult), reads=["tanb", "upbf"],
                        writes=[f"hm{bp}"])
                    p.op("dve", lambda e, nwd=nwd, bp=bp: e.tensor_mul(
                        w1b[bp][:, 0:nwd], gtbf[:, 0:nwd], w1b[bp][:, 0:nwd]),
                        reads=["gtbf", f"hm{bp}"], writes=[f"hm{bp}"])
                sks = []
                for bp, nn0, nwd, gb_, gk, ub_, uk in pcs:
                    s = OpSink()
                    ladder_quant("dve", f"hm{bp}", w1b[bp][:, 0:nwd], f"hm{bp}",
                                 w1b[bp][:, 0:nwd], nwd, QSC_B[bp], f"qb{bp}.",
                                 sink=s)
                    sks.append(s)
                zip_emit(*sks)
                for bp, nn0, nwd, gb_, gk, ub_, uk in pcs:
                    nic0 = nn0 // 128
                    nicw = nwd // 128
                    p.op("act", lambda e, nic0=nic0, nicw=nicw, nwd=nwd, sub=sub,
                         bp=bp: e.dma_start(
                        hmqT[:, sub, nic0:nic0 + nicw, :],
                        w1b[bp][:, 0:nwd].rearrange("p (a b) -> p a b", b=128),
                        transpose=True),
                        writes=[f"hm{bp}", "hmqT"], dma=True)
        h2_in(0)
        h2_lad()
        h2_out(0)
        h2_in(1)
        h2_lad()
        nch_block(0)
        h2_out(1)
        nch_block(1)
        if i + 1 < NT_B:
            lnb(i + 1)
        for foc in range(NFC):
            pa = foc % 2
            bank, bk = (pbR0, "R0") if pa == 0 else (pbR1, "R1")
            for ic in range(NIC):
                p.op("pe", lambda e, foc=foc, ic=ic, b=bank: e.matmul(
                    b[:, 0:256], wdT[:, ic, foc * 128:(foc + 1) * 128],
                    hmqT[:, :, ic, :], start=(ic == 0), stop=(ic == NIC - 1)),
                    reads=["hmqT", "wdT"], wclaim=((bk,) if ic == 0 else ()),
                    writes=([bk] if ic == NIC - 1 else []), inc=(ic == NIC - 1))
            p.op("dve", lambda e, foc=foc, b=bank, xmb=xmb: e.scalar_tensor_tensor(
                xmb[:, foc, :], b[:, 0:256], colv[:, 40 + foc:40 + foc + 1],
                xmb[:, foc, :], ALU.mult, ALU.add),
                reads=[bk, "colv", xmk], writes=[xmk])
        dst = y_fm[:, i * 256:(i + 1) * 256].rearrange("(fc p) t -> p fc t", p=128)
        p.op("sp", lambda e, d=dst, xmb=xmb: e.dma_start(d, xmb[:]),
             reads=[xmk], dma=True)

    p.barrier()
    p.emit(nc)
    return nc


_NC_CACHE = {}


def _get_nc(ntok):
    if ntok not in _NC_CACHE:
        _NC_CACHE[ntok] = build_nc(ntok)
    return _NC_CACHE[ntok]


def _consts():
    mask128 = np.zeros((128, CH), np.float32)
    for pp in range(128):
        mask128[pp, pp % 64:] = 1.0
    blkR16 = np.zeros((128, 128), ml_dtypes.bfloat16)
    sel16 = np.zeros((16, 1024), ml_dtypes.bfloat16)
    for fc in range(8):
        for pp in range(128):
            j = 2 * fc + (1 if pp >= 64 else 0)
            blkR16[pp, fc * 16 + j] = 1
            sel16[j, fc * 128 + pp] = 1
    return {
        "identb": np.eye(128, dtype=ml_dtypes.bfloat16),
        "mask128": mask128,
        "blkR16": blkR16,
        "sel16": sel16,
        "ones1": np.ones((128, 1), ml_dtypes.bfloat16),
        "onesb": np.ones((1, 128), np.float32),
        "zero512": np.zeros((128, 512), np.float32),
        "zero64b": np.zeros((128, CH), ml_dtypes.bfloat16),
    }


def kernel(L0_attn_x, L0_attn_Wq, L0_attn_Wk, L0_attn_Wv, L0_attn_Wo,
           L0_attn_Wf1, L0_attn_Wf2, L0_attn_Wg1, L0_attn_Wg2,
           L0_attn_norm_weight, L0_ffn_w_gate, L0_ffn_w_up, L0_ffn_w_down,
           L0_c, L0_wmod):
    x = np.asarray(L0_attn_x)
    B, ntok, d = x.shape
    assert d == D
    nc = _get_nc(ntok)
    shared = {
        "Wq": np.ascontiguousarray(L0_attn_Wq, np.float32),
        "Wk": np.ascontiguousarray(L0_attn_Wk, np.float32),
        "Wv": np.ascontiguousarray(L0_attn_Wv, np.float32),
        "Wo": np.ascontiguousarray(L0_attn_Wo, np.float32),
        "Wf1": np.ascontiguousarray(L0_attn_Wf1, np.float32),
        "Wf2": np.ascontiguousarray(L0_attn_Wf2, np.float32),
        "Wg1": np.ascontiguousarray(L0_attn_Wg1, np.float32),
        "Wg2": np.ascontiguousarray(L0_attn_Wg2, np.float32),
        "wmod": np.ascontiguousarray(L0_wmod, np.float32),
        "w_gate": np.ascontiguousarray(L0_ffn_w_gate, np.float32),
        "w_up": np.ascontiguousarray(L0_ffn_w_up, np.float32),
        "w_down": np.ascontiguousarray(L0_ffn_w_down, np.float32),
        "nw_col": np.ascontiguousarray(
            np.asarray(L0_attn_norm_weight, np.float32).reshape(NFC, 128).T),
    }
    shared.update(_consts())
    cc = np.asarray(L0_c, np.float32)
    in_maps = []
    for b in range(B):
        m = dict(shared)
        m["xT"] = np.ascontiguousarray(x[b].T.astype(np.float32, copy=False))
        m["c_col"] = np.ascontiguousarray(cc[b].reshape(NFC, 128).T)
        in_maps.append(m)
    res = run_bass_kernel_spmd(nc, in_maps, core_ids=list(range(B)))
    out = np.stack([np.ascontiguousarray(res.results[b]["y_fm"].T)
                    for b in range(B)], axis=0)
    return out


# version 21 (final)
# speedup vs baseline: 1.0157x; 1.0138x over previous
import numpy as np
import ml_dtypes

import concourse.bass as bass
import concourse.mybir as mybir
from concourse.bass_utils import run_bass_kernel_spmd

F32 = mybir.dt.float32
BF16 = mybir.dt.bfloat16
AF = mybir.ActivationFunctionType
ALU = mybir.AluOpType
AX = mybir.AxisListType

D = 1024
NFC = 8
HEADS = 16
HD = 64
CH = 64
FFI = 2816
NIC = FFI // 128
SCALE = HD ** -0.5
EPS = 1e-5


class Prog:

    ENGS = ("pe", "dve", "act", "pool", "sp")

    NSLOT = 4

    def __init__(self):
        self.sems = ["pe", "dve", "act_c", "pool_c"]
        for q in ("sp_q", "act_q", "pool_q"):
            for s in range(self.NSLOT):
                self.sems.append(f"{q}{s}")
        self.count = {s: 0 for s in self.sems}
        self.dmaidx = {"sp_q": 0, "act_q": 0, "pool_q": 0}
        self.stream = {e: [] for e in self.ENGS}
        self.writer = {}
        self.readers = {}
        self.pe_pending = False

    def _sem_for(self, eng, dma):
        if eng == "act":
            return "act_q" if dma else "act_c"
        if eng == "pool":
            return "pool_q" if dma else "pool_c"
        if eng == "sp":
            return "sp_q"
        return eng

    def op(self, eng, emit, reads=(), writes=(), dma=False, inc=True, wclaim=()):
        sem = self._sem_for(eng, dma)
        waits = {}
        if dma:
            q = sem
            slot = self.dmaidx[q] % self.NSLOT
            self.dmaidx[q] += 1
            sem = f"{q}{slot}"
            if self.count[sem] > 0:
                waits[sem] = self.count[sem]
        for k in reads:
            w = self.writer.get(k)
            if w:
                waits[w[0]] = max(waits.get(w[0], 0), w[1])
        for k in wclaim:
            w = self.writer.get(k)
            if w:
                waits[w[0]] = max(waits.get(w[0], 0), w[1])
            for s, t in self.readers.get(k, {}).items():
                waits[s] = max(waits.get(s, 0), t)
        for k in writes:
            w = self.writer.get(k)
            if w:
                waits[w[0]] = max(waits.get(w[0], 0), w[1])
            for s, t in self.readers.get(k, {}).items():
                waits[s] = max(waits.get(s, 0), t)
        step = 16 if dma else 1
        tok = self.count[sem] + step
        if inc:
            self.count[sem] = tok
            if eng == "pe":
                self.pe_pending = False
        else:
            assert eng == "pe" and not dma
            self.pe_pending = True
        for k in reads:
            self.readers.setdefault(k, {})[sem] = tok
        for k in writes:
            self.writer[k] = (sem, tok)
        self.stream[eng].append((waits, emit, sem if inc else None, step))

    def barrier(self):
        assert not self.pe_pending
        snap = dict(self.count)
        for e in self.ENGS:
            if e == "sp":
                sem, step = "sp_q0", 0
                self.stream[e].append((dict(snap), (lambda eng: eng.nop()), None, 0))
            else:
                sem = self._sem_for(e, False)
                self.count[sem] += 1
                self.stream[e].append(
                    (dict(snap), (lambda eng: eng.nop()), sem, 1))
        self.writer = {}
        self.readers = {}

    def emit(self, nc):
        from contextlib import ExitStack
        with ExitStack() as ctx:
            ctx.enter_context(
                nc.allow_low_precision(reason="bf16 decay/scan pipeline, budgeted"))
            semh = {}
            for s in self.sems:
                semh[s] = ctx.enter_context(nc.semaphore("sem_" + s))
            block = ctx.enter_context(nc.Block())
            eng_map = {"pe": block.tensor, "dve": block.vector, "act": block.scalar,
                       "pool": block.gpsimd, "sp": block.sync}
            for ename in self.ENGS:
                stream = self.stream[ename]
                if not stream:
                    continue
                def body(eng, stream=stream):
                    waited = {}
                    for waits, emitfn, semname, step in stream:
                        for s, v in waits.items():
                            if v > 0 and waited.get(s, 0) < v:
                                eng.wait_ge(semh[s], v)
                                waited[s] = v
                        ins = emitfn(eng)
                        if semname is not None:
                            ins.then_inc(semh[semname], step)
                eng_map[ename](body)


def build_nc(ntok, stop=None):
    NT_A = ntok // 512
    NT_B = ntok // 256
    NGC = 512 // CH

    nc = bass.Bass("TRN2", target_bir_lowering=False)

    xT = nc.dram_tensor("xT", [D, ntok], F32, kind="ExternalInput").ap()
    w_d = {nm: nc.dram_tensor(nm, [D, D], F32, kind="ExternalInput").ap()
           for nm in ("Wq", "Wk", "Wv", "Wo")}
    wf1_d = nc.dram_tensor("Wf1", [D, HD], F32, kind="ExternalInput").ap()
    wf2_d = nc.dram_tensor("Wf2", [HD, D], F32, kind="ExternalInput").ap()
    wg1_d = nc.dram_tensor("Wg1", [D, HD], F32, kind="ExternalInput").ap()
    wg2_d = nc.dram_tensor("Wg2", [HD, D], F32, kind="ExternalInput").ap()
    wmod_d = nc.dram_tensor("wmod", [D, 6 * D], F32, kind="ExternalInput").ap()
    wgate_d = nc.dram_tensor("w_gate", [FFI, D], F32, kind="ExternalInput").ap()
    wup_d = nc.dram_tensor("w_up", [FFI, D], F32, kind="ExternalInput").ap()
    wdown_d = nc.dram_tensor("w_down", [D, FFI], F32, kind="ExternalInput").ap()
    nw_col_d = nc.dram_tensor("nw_col", [128, NFC], F32, kind="ExternalInput").ap()
    c_col_d = nc.dram_tensor("c_col", [128, NFC], F32, kind="ExternalInput").ap()
    identb_d = nc.dram_tensor("identb", [128, 128], BF16, kind="ExternalInput").ap()
    mask_d = nc.dram_tensor("mask128", [128, CH], F32, kind="ExternalInput").ap()
    blkR_d = nc.dram_tensor("blkR16", [128, 128], BF16, kind="ExternalInput").ap()
    blk2_d = nc.dram_tensor("sel16", [16, 1024], BF16, kind="ExternalInput").ap()
    ones1_d = nc.dram_tensor("ones1", [128, 1], BF16, kind="ExternalInput").ap()
    onesb_d = nc.dram_tensor("onesb", [1, 128], F32, kind="ExternalInput").ap()
    zero512_d = nc.dram_tensor("zero512", [128, 512], F32, kind="ExternalInput").ap()
    zero64b_d = nc.dram_tensor("zero64b", [128, CH], BF16, kind="ExternalInput").ap()

    y_fm = nc.dram_tensor("y_fm", [D, ntok], F32, kind="ExternalOutput").ap()
    xmid = nc.dram_tensor("xmid", [D, ntok], F32, kind="Internal").ap()
    sgate = nc.dram_tensor("sgate", [FFI, D], BF16, kind="Internal").ap()
    sup = nc.dram_tensor("sup", [FFI, D], BF16, kind="Internal").ap()
    sdown = nc.dram_tensor("sdown", [D, FFI], BF16, kind="Internal").ap()

    pbR0 = nc.alloc_psum_tensor("pbR0", [128, 512], F32)
    pbR1 = nc.alloc_psum_tensor("pbR1", [128, 512], F32)
    pbO = nc.alloc_psum_tensor("pbO", [128, 512], F32)
    pbPT = nc.alloc_psum_tensor("pbPT", [128, 1024], F32)
    pbKH = nc.alloc_psum_tensor("pbKH", [128, 1024], BF16)
    pbST = nc.alloc_psum_tensor("pbST", [128, 1024], F32)

    A_ = nc.alloc_sbuf_tensor
    mask = A_("mask", [128, CH], F32)
    blkR = A_("s_blkR16", [128, 128], BF16)
    blk2 = A_("s_sel16", [16, 1024], BF16)
    ones1 = A_("s_ones1", [128, 1], BF16)
    onesb = A_("s_onesb", [1, 128], F32)
    zero64b = A_("s_zero64b", [128, CH], BF16)
    nw_col = A_("s_nw_col", [128, NFC], F32)
    c_col = A_("s_c_col", [128, NFC], F32)
    sc_col = A_("sc_col", [128, NFC], F32)
    modcol = A_("modcol", [128, 48], F32)
    colv = A_("colv", [128, 6 * NFC], F32)
    ACf32 = A_("ACf32", [128, NFC * 8], F32)
    tcol = A_("tcol", [128, 2 * NFC], F32)
    lnstA = A_("lnstA", [128, 2048], F32)
    mstat = lnstA[0:1, 0:512]
    istat_f = lnstA[0:1, 512:1024]
    vstat = lnstA[32:33, 0:512]
    lsc0 = lnstA[32:33, 512:1024]
    lsc1 = lnstA[32:33, 1024:1536]
    istat = lnstA[32:33, 1536:2048]
    rst = lnstA[64:80, 0:512]
    rs0 = lnstA[64:80, 512:1024]
    rs1 = lnstA[64:80, 1024:1536]
    rbf = A_("rbf", [16, 512], BF16)
    rbc = A_("rbc", [128, 512], BF16)
    S_sb = A_("S_sb", [128, NFC, HD], F32)

    ARENA = 196512
    arena = A_("arena", [128, ARENA], mybir.dt.uint8)
    arena_addr = nc.lookup_mloc(arena).addr
    cur = [0]

    arena_off = {}

    def aalloc(name, shape, dtype, at=None):
        per = 1
        for s in shape[1:]:
            per *= s
        per *= mybir.dt.size(dtype)
        if at is not None:
            off = at
        else:
            off = (cur[0] + 31) // 32 * 32
            cur[0] = off + per
            assert cur[0] <= ARENA, f"arena overflow at {name}: {cur[0]}"
        h = nc.alloc_sbuf_tensor_at(name, shape, dtype, offset=arena_addr + off)
        arena_off[name] = off
        return h

    wq = aalloc("wq", [128, NFC, D], BF16)
    wk = aalloc("wk", [128, NFC, D], BF16)
    wv = aalloc("wv", [128, NFC, D], BF16)
    wo = aalloc("wo", [128, NFC, D], BF16)
    wf1 = aalloc("wf1", [128, NFC, HD], BF16)
    wf2 = aalloc("wf2", [HD, NFC, 128], BF16)
    wg1 = aalloc("wg1", [128, NFC, HD], BF16)
    wg2 = aalloc("wg2", [HD, NFC, 128], BF16)
    xt = [aalloc(f"xt{i}", [128, NFC, 512], F32) for i in range(2)]
    h_sb = aalloc("h_sb", [128, NFC, 512], BF16)
    xbs = [aalloc(f"xbs{i}", [128, 512], BF16) for i in range(2)]
    xqs = [aalloc(f"xqs{i}", [128, 512], BF16) for i in range(2)]
    t0s = aalloc("t0s", [128, 512], F32)
    t1s = aalloc("t1s", [128, 512], F32)
    tanhs = [aalloc(f"tanhs{i}", [128, 512], BF16) for i in range(2)]
    u1s = [aalloc(f"u1s{i}", [128, 512], BF16) for i in range(2)]
    zvs = [aalloc(f"zvs{i}", [128, 512], BF16) for i in range(2)]
    fs = [aalloc(f"fs{i}", [128, 512], BF16) for i in range(2)]
    u1f_sb = aalloc("u1f_sb", [HD, 512], BF16)
    u1g_sb = aalloc("u1g_sb", [HD, 512], BF16)
    g1s = [aalloc(f"g1s{i}", [128, 512], BF16) for i in range(2)]
    invA = aalloc("invA", [128, NFC, 512], BF16)
    A_sb = aalloc("A_sb", [128, NFC, 512], BF16)
    qT = aalloc("qT", [128, NFC, 512], BF16)
    kT = aalloc("kT", [128, NFC, 512], BF16)
    tg_sb = aalloc("tg_sb", [128, NFC, 512], BF16)
    v_tok = aalloc("v_tok", [128, 4, D], BF16)
    og = aalloc("og_al", [128, NFC, 512], BF16, at=arena_off["invA"])
    ACbf = aalloc("ACbf", [128, NFC, NGC], BF16)
    Sbf = aalloc("ln.t0", [128, NFC, HD], BF16, at=arena_off["t0s"] + 1024)
    khtok_tile = aalloc("khtok_tile", [128, 4, D], BF16, at=arena_off["A_sb"])
    ptm = aalloc("ptm", [128, 2, NFC, CH], BF16, at=arena_off["xqs0"])
    sqs = aalloc("sqs", [128, NFC, CH], BF16, at=arena_off["t0s"])
    obf = aalloc("ln.t1", [128, NFC, CH], BF16, at=arena_off["t1s"])
    khT_pair = aalloc("khT_pair", [128, NFC, 128], BF16, at=arena_off["xbs0"])
    phaseA_end = cur[0]

    phaseA_full = cur[0]
    assert phaseA_full <= ARENA, phaseA_full

    cur[0] = 0
    wgT = aalloc("wgT", [128, NFC, FFI], BF16)
    wuT = aalloc("wuT", [128, NFC, FFI], BF16)
    wdT = aalloc("wdT", [128, NIC, D], BF16)
    xm = aalloc("xm", [128, NFC, 256], F32)
    xm2 = aalloc("xm2", [128, NFC, 256], F32, at=184224)
    h2_sb = aalloc("h2_sb", [128, 2, NFC, 128], BF16)
    h2tok = aalloc("h2tok", [128, D], BF16)
    h2qT = aalloc("h2qT", [128, 2, NFC, 128], BF16)
    hmqT = aalloc("hmqT", [128, 2, NIC, 128], BF16)
    xbs_b = [aalloc(f"xbsb{i}", [128, 256], BF16) for i in range(2)]
    xqs_b = [aalloc(f"xqsb{i}", [128, 256], BF16) for i in range(2)]
    t0b = aalloc("t0b", [128, 256], F32)
    t1b = aalloc("t1b", [128, 256], F32)
    tanb = aalloc("tanb", [128, 512], BF16)
    upbf = aalloc("upbf", [128, 512], BF16)
    gtbf = aalloc("gtbf", [128, 512], BF16)
    w1b = [aalloc(f"w1b{i}", [128, 512], BF16) for i in range(2)]
    QSC_B = []
    for bp in range(2):
        QSC_B.append({
            "a": aalloc(f"b{bp}_a", [128, 512], BF16),
            "t": aalloc(f"b{bp}_t", [128, 512], BF16),
            "v2": aalloc(f"b{bp}_v2", [128, 512], BF16),
            "sg": aalloc(f"b{bp}_sg", [128, 512], BF16),
            "bm": aalloc(f"b{bp}_bm", [128, 512 // 16], F32),
            "sc2": aalloc(f"b{bp}_sc2", [128, 512 // 16], F32),
            "sc2b": aalloc(f"b{bp}_sc2b", [128, 512 // 16], BF16),
        })
    phaseB_full = cur[0]
    assert phaseB_full <= ARENA, phaseB_full

    QSC_P, KEYMAP_P, WPIN_P, WPOUT_P = [], [], [], []
    for kpar in range(2):
        host = "qT" if kpar == 0 else "kT"
        base = arena_off[host]
        qsc = {
            "a": aalloc(f"qp{kpar}_a", [128, 1024], BF16, at=base),
            "t": aalloc(f"qp{kpar}_t", [128, 1024], BF16, at=base + 2048),
            "v2": aalloc(f"qp{kpar}_v2", [128, 1024], BF16, at=base + 4096),
            "sg": aalloc(f"qp{kpar}_sg", [128, 1024], BF16, at=base + 6144),
            "bm": aalloc(f"qp{kpar}_bm", [128, 64], F32,
                         at=182944 + kpar * 640),
            "sc2": aalloc(f"qp{kpar}_sc2", [128, 64], F32,
                          at=182944 + kpar * 640 + 256),
            "sc2b": aalloc(f"qp{kpar}_sc2b", [128, 64], BF16,
                           at=182944 + kpar * 640 + 512),
        }
        QSC_P.append(qsc)
        KEYMAP_P.append({"a": host, "t": host, "v2": host, "sg": host,
                         "bm": f"pst{kpar}", "sc": f"pst{kpar}",
                         "scb": f"pst{kpar}"})
        WPIN_P.append(aalloc(f"wpinp{kpar}", [128, 1024], BF16,
                             at=192416 + kpar * 2048))
        WPOUT_P.append(aalloc(f"wpoutp{kpar}", [128, 1024], BF16,
                              at=arena_off["tg_sb"] + kpar * 2048))

    p = Prog()

    def _finish_early():
        p.barrier()
        for i in range(NT_B):
            dst = y_fm[:, i * 256:(i + 1) * 256].rearrange("(fc p) t -> p fc t", p=128)
            p.op("sp", lambda e, d=dst: e.dma_start(d, xm[:]), reads=["xm0"],
                 dma=True)
        p.barrier()
        p.emit(nc)
        return nc

    class OpSink:
        def __init__(self):
            self.ops = []

        def op(self, *a, **k):
            self.ops.append((a, k))

    def zip_emit(*sinks):
        idx = 0
        while True:
            left = False
            for s in sinks:
                if idx < len(s.ops):
                    a, k = s.ops[idx]
                    p.op(*a, **k)
                    left = True
            if not left:
                break
            idx += 1

    def ladder_quant(eng2, src_key, src_ap, out_key, out_ap, width, qsc, kpfx,
                     key_map=None, eng3="dve", sink=None):
        q = sink if sink is not None else p
        nblk = width // 16
        a, t, v2, sg = qsc["a"], qsc["t"], qsc["v2"], qsc["sg"]
        bm, sc2, sc2b = qsc["bm"], qsc["sc2"], qsc["sc2b"]
        km = key_map or {}
        ka = km.get("a", kpfx + "a")
        kt = km.get("t", kpfx + "t")
        kv2 = km.get("v2", kpfx + "v2")
        ksg = km.get("sg", kpfx + "sg")
        kbm = km.get("bm", kpfx + "bm")
        ksc = km.get("sc", kpfx + "sc")
        kscb = km.get("scb", kpfx + "scb")
        src3 = src_ap.rearrange("p (n b) -> p n b", b=16)
        q.op("dve", lambda e: e.reduce_max(
            bm[:, 0:nblk].rearrange("p n -> p n ()"), src3, axis=AX.X,
            apply_absolute_value=True), reads=[src_key], writes=[kbm])
        q.op("dve", lambda e: e.tensor_scalar(
            sc2[:, 0:nblk], bm[:, 0:nblk], 1.0 / 12.0, 5e-13, ALU.mult, ALU.max),
            reads=[kbm], writes=[ksc])
        q.op("dve", lambda e: e.reciprocal(bm[:, 0:nblk], sc2[:, 0:nblk]),
             reads=[ksc], writes=[kbm])
        q.op("dve", lambda e: e.tensor_scalar(
            sc2b[:, 0:nblk], bm[:, 0:nblk], 0.5, None, ALU.mult),
            reads=[kbm], writes=[kscb])
        q.op(eng3, lambda e: e.tensor_tensor(
            a[:, 0:width].rearrange("p (n b) -> p n b", b=16), src3,
            sc2b[:, 0:nblk].rearrange("p n -> p n ()").broadcast_to([128, nblk, 16]),
            ALU.mult), reads=[src_key, kscb], writes=[ka])
        q.op("act", lambda e: e.activation(sg[:, 0:width], a[:, 0:width], AF.Sign),
             reads=[ka], writes=[ksg])
        q.op(eng2, lambda e: e.tensor_scalar(
            v2[:, 0:width], a[:, 0:width], -1.0, None, ALU.mult),
            reads=[ka], writes=[kv2])
        q.op(eng2, lambda e: e.tensor_tensor(
            a[:, 0:width], a[:, 0:width], v2[:, 0:width], ALU.max),
            reads=[ka, kv2], writes=[ka])
        q.op(eng2, lambda e: e.tensor_scalar(
            t[:, 0:width], a[:, 0:width], 0.25, None, ALU.is_gt),
            reads=[ka], writes=[kt])
        for mid in (0.75, 1.25, 1.75):
            q.op(eng2, lambda e, m=mid: e.scalar_tensor_tensor(
                t[:, 0:width], a[:, 0:width], m, t[:, 0:width], ALU.is_gt, ALU.add),
                reads=[ka, kt], writes=[kt])
        q.op(eng2, lambda e: e.tensor_scalar(
            v2[:, 0:width], a[:, 0:width], 2.5, None, ALU.is_gt),
            reads=[ka], writes=[kv2])
        q.op(eng2, lambda e: e.scalar_tensor_tensor(
            v2[:, 0:width], a[:, 0:width], 3.5, v2[:, 0:width], ALU.is_gt, ALU.add),
            reads=[ka, kv2], writes=[kv2])
        q.op(eng2, lambda e: e.scalar_tensor_tensor(
            t[:, 0:width], v2[:, 0:width], 2.0, t[:, 0:width], ALU.mult, ALU.add),
            reads=[kv2, kt], writes=[kt])
        q.op(eng2, lambda e: e.tensor_scalar(
            v2[:, 0:width], a[:, 0:width], 5.0, None, ALU.is_gt),
            reads=[ka], writes=[kv2])
        q.op(eng2, lambda e: e.scalar_tensor_tensor(
            t[:, 0:width], v2[:, 0:width], 4.0, t[:, 0:width], ALU.mult, ALU.add),
            reads=[kv2, kt], writes=[kt])
        q.op(eng3, lambda e: e.tensor_tensor(
            t[:, 0:width], t[:, 0:width], sg[:, 0:width], ALU.mult),
            reads=[kt, ksg], writes=[kt])
        q.op("dve", lambda e: e.tensor_copy(sc2b[:, 0:nblk], sc2[:, 0:nblk]),
             reads=[ksc], writes=[kscb])
        q.op(eng3, lambda e: e.tensor_tensor(
            out_ap.rearrange("p (n b) -> p n b", b=16),
            t[:, 0:width].rearrange("p (n b) -> p n b", b=16),
            sc2b[:, 0:nblk].rearrange("p n -> p n ()").broadcast_to([128, nblk, 16]),
            ALU.mult), reads=[kt, kscb], writes=[out_key])

    def rsqrt_via_sqrt(pfx, dst_ap, src_ap, scrA, scrB, rows, cols):
        p.op("act", lambda e: e.activation(scrA, src_ap, AF.Sqrt),
             reads=[pfx + "v"], writes=[pfx + "s0"])
        p.op("dve", lambda e: e.reciprocal(dst_ap, scrA),
             reads=[pfx + "s0"], writes=[pfx + "r"])
        p.op("dve", lambda e: e.tensor_mul(scrB, dst_ap, dst_ap),
             reads=[pfx + "r"], writes=[pfx + "t"])
        p.op("dve", lambda e: e.tensor_mul(scrB, scrB, src_ap),
             reads=[pfx + "t", pfx + "v"], writes=[pfx + "t"])
        p.op("dve", lambda e: e.tensor_scalar(scrB, scrB, -0.5, 1.5, ALU.mult, ALU.add),
             reads=[pfx + "t"], writes=[pfx + "t"])
        p.op("dve", lambda e: e.tensor_mul(dst_ap, dst_ap, scrB),
             reads=[pfx + "r", pfx + "t"], writes=[pfx + "r"])

    def layernorm(xbuf, xkey, T, scol_off, hout, hkey, xbp, xqp, t0x, t1x,
                  tin=None):
        for fc in range(NFC):
            pa = fc % 2
            p.op("dve", lambda e, fc=fc, pa=pa: e.tensor_copy(
                xbp[pa][:, 0:T], xbuf[:, fc, 0:T]), reads=[xkey], writes=[f"xb{pa}"])
            p.op("act", lambda e, pa=pa: e.activation(
                xqp[pa][:, 0:T], xbp[pa][:, 0:T], AF.Square),
                reads=[f"xb{pa}"], writes=[f"xq{pa}"])
            p.op("pe", lambda e, pa=pa, fc=fc: e.matmul(
                pbST[0:1, 0:T], ones1[:], xbp[pa][:, 0:T],
                start=(fc == 0), stop=(fc == NFC - 1)),
                reads=[f"xb{pa}", "ones1"], wclaim=(("ST",) if fc == 0 else ()),
                writes=[], inc=True)
            p.op("pe", lambda e, pa=pa, fc=fc: e.matmul(
                pbST[0:1, 512:512 + T], ones1[:], xqp[pa][:, 0:T],
                start=(fc == 0), stop=(fc == NFC - 1)),
                reads=[f"xq{pa}", "ones1"], wclaim=(("ST",) if fc == 0 else ()),
                writes=(["ST"] if fc == NFC - 1 else []), inc=True)
        p.op("dve", lambda e: e.tensor_scalar(
            mstat[0:1, 0:T], pbST[0:1, 0:T], 1.0 / D, None, ALU.mult),
            reads=["ST"], writes=["ln.m"])
        p.op("dve", lambda e: e.tensor_scalar(
            vstat[0:1, 0:T], pbST[0:1, 512:512 + T], 1.0 / D, EPS, ALU.mult, ALU.add),
            reads=["ST"], writes=["ln.v"])
        p.op("dve", lambda e: e.tensor_mul(lsc0[0:1, 0:T], mstat[0:1, 0:T],
                                           mstat[0:1, 0:T]),
             reads=["ln.m"], writes=["ln.m2"])
        p.op("dve", lambda e: e.tensor_sub(vstat[0:1, 0:T], vstat[0:1, 0:T],
                                           lsc0[0:1, 0:T]),
             reads=["ln.v", "ln.m2"], writes=["ln.v"])
        rsqrt_via_sqrt("ln.", istat[0:1, 0:T], vstat[0:1, 0:T],
                       lsc0[0:1, 0:T], lsc1[0:1, 0:T], 1, T)
        p.op("dve", lambda e: e.tensor_copy(istat_f[0:1, 0:T], istat[0:1, 0:T]),
             reads=["ln.r"], writes=["ln.bc2"])
        p.op("pe", lambda e: e.matmul(pbR0[:, 0:T], onesb[:], mstat[0:1, 0:T],
                                      start=True, stop=True),
             reads=["ln.m", "onesb"], writes=["R0"])
        p.op("pe", lambda e: e.matmul(pbR1[:, 0:T], onesb[:], istat_f[0:1, 0:T],
                                      start=True, stop=True),
             reads=["ln.bc2", "onesb"], writes=["R1"])
        for fc in range(NFC):
            tx, tk = (t0x, "ln.t0") if fc % 2 == 0 else (t1x, "ln.t1")
            p.op("dve", lambda e, fc=fc, tx=tx: e.tensor_sub(
                tx[:, 0:T], xbuf[:, fc, 0:T], pbR0[:, 0:T]),
                reads=[xkey, "R0"], writes=[tk])
            p.op("dve", lambda e, tx=tx: e.tensor_mul(
                tx[:, 0:T], tx[:, 0:T], pbR1[:, 0:T]),
                reads=[tk, "R1"], writes=[tk])
            p.op("dve", lambda e, fc=fc, tx=tx: e.tensor_scalar(
                hout(fc), (tin(tx) if tin else tx[:, 0:T]),
                colv[:, scol_off + 8 + fc:scol_off + 8 + fc + 1],
                colv[:, scol_off + fc:scol_off + fc + 1],
                ALU.mult, ALU.add), reads=[tk, "colv"], writes=[hkey])

    for dst, src, key in [(mask, mask_d, "mask"),
                          (blkR, blkR_d, "blkR"), (blk2, blk2_d, "blk2"),
                          (ones1, ones1_d, "ones1"), (onesb, onesb_d, "onesb"),
                          (zero64b, zero64b_d, "zero"), (nw_col, nw_col_d, "nw"),
                          (c_col, c_col_d, "c_col")]:
        p.op("sp", lambda e, d=dst, s=src: e.dma_start(d[:], s), writes=[key], dma=True)
    p.op("sp", lambda e: e.dma_start(S_sb[:].rearrange("p a b -> p (a b)"),
                                     zero512_d), writes=["S"], dma=True)
    for dst, src, key in [(wf1, wf1_d, "wf1"), (wf2, wf2_d, "wf2"),
                          (wq, w_d["Wq"], "wq"), (wk, w_d["Wk"], "wk"),
                          (wv, w_d["Wv"], "wv"), (wg1, wg1_d, "wg1"),
                          (wg2, wg2_d, "wg2"), (wo, w_d["Wo"], "wo")]:
        if key in ("wf2", "wg2"):
            rs = src.rearrange("k (fc f) -> k fc f", fc=NFC)
        else:
            rs = src.rearrange("(kc p) f -> p kc f", p=128)
        p.op("pool", lambda e, d=dst, s=rs: e.dma_start(d[:], s), writes=[key], dma=True)

    if stop == "consts":
        return _finish_early()
    p.op("act", lambda e: e.activation(tcol[:, 0:NFC], c_col[:], AF.Tanh, scale=0.5),
         reads=["c_col"], writes=["tanc"])
    p.op("dve", lambda e: e.scalar_tensor_tensor(
        tcol[:, 0:NFC], tcol[:, 0:NFC], 1.0, c_col[:], ALU.add, ALU.mult),
        reads=["tanc", "c_col"], writes=["tanc"])
    p.op("dve", lambda e: e.tensor_scalar(sc_col[:], tcol[:, 0:NFC], 0.5, None,
                                          ALU.mult), reads=["tanc"], writes=["sc_col"])

    wmst = [aalloc(f"wmst{i}", [128, NFC, 512], F32, at=arena_off[f"xt{i}"])
            for i in range(2)]
    for foc in range(12):
        pa = foc % 2
        src = wmod_d[:, foc * 512:(foc + 1) * 512].rearrange("(kc p) f -> p kc f", p=128)
        p.op("sp" if pa == 0 else "act", lambda e, s=src, pa=pa: e.dma_start(
            wmst[pa][:], s), writes=[f"xt{pa}"], dma=True)
        for sub in range(4):
            col = foc * 4 + sub
            for kc in range(NFC):
                p.op("pe", lambda e, pa=pa, sub=sub, kc=kc, col=col: e.matmul(
                    pbPT[:, col:col + 1], wmst[pa][:, kc, sub * 128:(sub + 1) * 128],
                    sc_col[:, kc:kc + 1], start=(kc == 0), stop=(kc == NFC - 1)),
                    reads=[f"xt{pa}", "sc_col"], wclaim=(("PT",) if kc == 0 else ()),
                    writes=(["PT"] if kc == NFC - 1 else []), inc=(kc == NFC - 1))
    p.op("dve", lambda e: e.tensor_copy(modcol[:], pbPT[:, 0:48]),
         reads=["PT"], writes=["modcol"])
    p.op("dve", lambda e: e.tensor_copy(colv[:, 0:8], modcol[:, 0:8]),
         reads=["modcol"], writes=["colv"])
    p.op("dve", lambda e: e.tensor_scalar(colv[:, 8:16], modcol[:, 8:16], 1.0, None,
                                          ALU.add), reads=["modcol"], writes=["colv"])
    p.op("dve", lambda e: e.tensor_copy(colv[:, 24:32], modcol[:, 24:32]),
         reads=["modcol"], writes=["colv"])
    p.op("dve", lambda e: e.tensor_scalar(colv[:, 32:40], modcol[:, 32:40], 1.0, None,
                                          ALU.add), reads=["modcol"], writes=["colv"])
    for co, ci in ((16, 16), (40, 40)):
        p.op("dve", lambda e, ci=ci: e.tensor_scalar(
            tcol[:, 8:16], modcol[:, ci:ci + 8], -1.0, None, ALU.mult),
            reads=["modcol"], writes=["gneg"])
        p.op("dve", lambda e, co=co, ci=ci: e.tensor_tensor(
            colv[:, co:co + 8], modcol[:, ci:ci + 8], tcol[:, 8:16], ALU.max),
            reads=["modcol", "gneg"], writes=["colv"])
        p.op("dve", lambda e, co=co: e.tensor_scalar(
            colv[:, co:co + 8], colv[:, co:co + 8], 0.25, None, ALU.max),
            reads=["colv"], writes=["colv"])
        p.op("dve", lambda e, ci=ci: e.tensor_scalar(
            tcol[:, 0:8], modcol[:, ci:ci + 8], 0.0, 2.0, ALU.is_ge, ALU.mult),
            reads=["modcol"], writes=["gsn"])
        p.op("dve", lambda e: e.tensor_scalar(tcol[:, 0:8], tcol[:, 0:8], -1.0, None,
                                              ALU.add), reads=["gsn"], writes=["gsn"])
        p.op("dve", lambda e, co=co: e.tensor_mul(colv[:, co:co + 8],
                                                  colv[:, co:co + 8], tcol[:, 0:8]),
             reads=["colv", "gsn"], writes=["colv"])

    if stop == "mod":
        return _finish_early()
    prep_tasks = []
    prep_units = []

    def _mk_prep(src_d, dst_d, nrows, K):
        pieces = [(0, 1024), (1024, 1024), (2048, 768)] if K == FFI else [(0, K)]
        for r in range(nrows):
            for (k0, kw) in pieces:
                prep_units.append((src_d, dst_d, r, k0, kw))
    _mk_prep(wgate_d, sgate, NIC, D)
    _mk_prep(wup_d, sup, NIC, D)
    _mk_prep(wdown_d, sdown, NFC, FFI)

    for _ui in range(0, len(prep_units), 2):
        pair = [(kp, prep_units[_ui + kp]) for kp in range(2)
                if _ui + kp < len(prep_units)]

        def task(pair=pair):
            for kp, (src_d, dst_d, r, k0, kw) in pair:
                srcap = src_d[r * 128:(r + 1) * 128, k0:k0 + kw]
                p.op("pool", lambda e, s=srcap, kp=kp, kw=kw: e.dma_start(
                    WPIN_P[kp][:, 0:kw], s), writes=[f"wpin{kp}"], dma=True)
            sks = []
            for kp, (src_d, dst_d, r, k0, kw) in pair:
                s = OpSink()
                ladder_quant("dve", f"wpin{kp}", WPIN_P[kp][:, 0:kw], "tg",
                             WPOUT_P[kp][:, 0:kw], kw, QSC_P[kp], "qw.",
                             key_map=KEYMAP_P[kp], sink=s)
                sks.append(s)
            zip_emit(*sks)
            for kp, (src_d, dst_d, r, k0, kw) in pair:
                dst = dst_d[r * 128:(r + 1) * 128, k0:k0 + kw]
                p.op("act", lambda e, d=dst, kp=kp, kw=kw: e.dma_start(
                    d, WPOUT_P[kp][:, 0:kw]), reads=["tg"], dma=True)
        prep_tasks.append(task)

    if stop == "prep":
        for t in prep_tasks:
            t()
        prep_tasks = []
        return _finish_early()

    def load_xtile(i):
        pa = i % 2
        src = xT[:, i * 512:(i + 1) * 512].rearrange("(fc p) t -> p fc t", p=128)
        p.op("sp", lambda e, s=src, pa=pa: e.dma_start(xt[pa][:], s),
             writes=[f"xt{pa}"], dma=True)

    def lnA(i):
        layernorm(xt[i % 2][:], f"xt{i % 2}", 512, 0,
                  lambda fc: h_sb[:, fc, 0:512], "h", xbs, xqs, t0s, t1s)

    load_xtile(0)
    lnA(0)
    for i in range(NT_A):
        pa_t = i % 2
        if i + 1 < NT_A:
            load_xtile(i + 1)
        xk = f"xt{pa_t}"
        xb = xt[pa_t]

        if stop == "A_ln":
            return _finish_early()

        for kc in range(NFC):
            p.op("pe", lambda e, kc=kc: e.matmul(
                pbR0[0:HD, 0:512], wf1[:, kc, :], h_sb[:, kc, :],
                start=(kc == 0), stop=(kc == NFC - 1)),
                reads=["h", "wf1"], wclaim=(("R0",) if kc == 0 else ()),
                writes=(["R0"] if kc == NFC - 1 else []), inc=(kc == NFC - 1))
        p.op("dve", lambda e: e.tensor_copy(u1f_sb[:, :], pbR0[0:HD, 0:512]),
             reads=["R0"], writes=["u1f"])
        for foc in range(NFC):
            pa = foc % 2
            bank, bk = (pbR0, "R0") if pa == 0 else (pbR1, "R1")
            p.op("pe", lambda e, foc=foc, b=bank: e.matmul(
                b[:, 0:512], wf2[:, foc, :], u1f_sb[:, :], start=True, stop=True),
                reads=["u1f", "wf2"], writes=[bk])
            p.op("act", lambda e, b=bank, pa=pa: e.activation(
                g1s[pa][:, :], b[:, 0:512], AF.Exp, scale=-1.0),
                reads=[bk], writes=[f"g1{pa}"])
            p.op("act", lambda e, b=bank, pa=pa: e.activation(
                fs[pa][:, :], b[:, 0:512], AF.Tanh, scale=0.5),
                reads=[bk], writes=[f"fs{pa}"])
            p.op("dve", lambda e, pa=pa: e.tensor_scalar(
                g1s[pa][:, :], g1s[pa][:, :], 1.0, None, ALU.add),
                reads=[f"g1{pa}"], writes=[f"g1{pa}"])
            p.op("dve", lambda e, pa=pa: e.tensor_scalar(
                fs[pa][:, :], fs[pa][:, :], 0.5, 0.5, ALU.mult, ALU.add),
                reads=[f"fs{pa}"], writes=[f"fs{pa}"])
            for gc in range(NGC):
                p.op("dve", lambda e, foc=foc, gc=gc, pa=pa: e.tensor_tensor_scan(
                    invA[:, foc, gc * CH:(gc + 1) * CH],
                    g1s[pa][:, gc * CH:(gc + 1) * CH], zero64b[:], 1.0,
                    ALU.mult, ALU.add),
                    reads=[f"g1{pa}", "zero"], writes=["invA"])
                p.op("dve", lambda e, foc=foc, gc=gc, pa=pa: e.tensor_tensor_scan(
                    A_sb[:, foc, gc * CH:(gc + 1) * CH],
                    fs[pa][:, gc * CH:(gc + 1) * CH], zero64b[:], 1.0,
                    ALU.mult, ALU.add),
                    reads=[f"fs{pa}", "zero"], writes=["Agrp"])
        p.op("dve", lambda e: e.tensor_copy(
            ACbf[:], A_sb[:].rearrange("p a (g c) -> p a g c", c=CH)[:, :, :, CH - 1]),
            reads=["Agrp"], writes=["AC"])
        p.op("dve", lambda e: e.tensor_copy(
            ACf32[:].rearrange("p (a g) -> p a g", g=NGC),
            A_sb[:].rearrange("p a (g c) -> p a g c", c=CH)[:, :, :, CH - 1]),
            reads=["Agrp"], writes=["ACf"])

        if stop == "A_f":
            return _finish_early()
        for (wmat, wkey, sc_buf, sckey, dbuf, dkey, zsc) in (
                (wq, "wq", A_sb, "Agrp", qT, "qT", 0.5 * SCALE),
                (wk, "wk", invA, "invA", kT, "kT", 0.5)):
            for foc in range(NFC):
                pa = foc % 2
                bank, bk = (pbR0, "R0") if pa == 0 else (pbR1, "R1")
                for kc in range(NFC):
                    p.op("pe", lambda e, foc=foc, kc=kc, b=bank, w=wmat: e.matmul(
                        b[:, 0:512], w[:, kc, foc * 128:(foc + 1) * 128],
                        h_sb[:, kc, :], start=(kc == 0), stop=(kc == NFC - 1)),
                        reads=["h", wkey], wclaim=((bk,) if kc == 0 else ()),
                        writes=([bk] if kc == NFC - 1 else []), inc=(kc == NFC - 1))
                p.op("act", lambda e, b=bank, pa=pa: e.activation(
                    tanhs[pa][:, :], b[:, 0:512], AF.Tanh, scale=0.5),
                    reads=[bk], writes=[f"tanh{pa}"])
                p.op("act", lambda e, b=bank, z=zsc, pa=pa: e.activation(
                    zvs[pa][:, :], b[:, 0:512], AF.Copy, scale=z),
                    reads=[bk], writes=[f"zv{pa}"])
                p.op("dve", lambda e, foc=foc, sb_=sc_buf, pa=pa:
                     e.scalar_tensor_tensor(
                    u1s[pa][:, :], tanhs[pa][:, :], 1.0, sb_[:, foc, :],
                    ALU.add, ALU.mult),
                    reads=[f"tanh{pa}", sckey], writes=[f"u1{pa}"])
                p.op("dve", lambda e, foc=foc, d=dbuf, pa=pa: e.tensor_mul(
                    d[:, foc, :], zvs[pa][:, :], u1s[pa][:, :]),
                    reads=[f"zv{pa}", f"u1{pa}"], writes=[dkey])

        if stop == "A_qk":
            return _finish_early()
        for sub in range(4):
            for half in range(2):
                bank, bk = (pbPT, "PT") if half == 0 else (pbO, "O")
                for kc in range(NFC):
                    p.op("pe", lambda e, sub=sub, half=half, kc=kc, b=bank: e.matmul(
                        b[:, 0:512], h_sb[:, kc, sub * 128:(sub + 1) * 128],
                        wv[:, kc, half * 512:(half + 1) * 512],
                        start=(kc == 0), stop=(kc == NFC - 1)),
                        reads=["h", "wv"], wclaim=((bk,) if kc == 0 else ()),
                        writes=([bk] if kc == NFC - 1 else []), inc=(kc == NFC - 1))
                p.op("dve", lambda e, sub=sub, half=half, b=bank: e.tensor_copy(
                    v_tok[:, sub, half * 512:(half + 1) * 512], b[:, 0:512]),
                    reads=[bk], writes=["v"])

        if stop == "A_v":
            return _finish_early()
        for kc in range(NFC):
            p.op("pe", lambda e, kc=kc: e.matmul(
                pbR0[0:HD, 0:512], wg1[:, kc, :], h_sb[:, kc, :],
                start=(kc == 0), stop=(kc == NFC - 1)),
                reads=["h", "wg1"], wclaim=(("R0",) if kc == 0 else ()),
                writes=(["R0"] if kc == NFC - 1 else []), inc=(kc == NFC - 1))
        p.op("dve", lambda e: e.tensor_copy(u1g_sb[:, :], pbR0[0:HD, 0:512]),
             reads=["R0"], writes=["u1g"])
        for foc in range(NFC):
            pa = foc % 2
            bank, bk = (pbR0, "R0") if pa == 0 else (pbR1, "R1")
            p.op("pe", lambda e, foc=foc, b=bank: e.matmul(
                b[:, 0:512], wg2[:, foc, :], u1g_sb[:, :], start=True, stop=True),
                reads=["u1g", "wg2"], writes=[bk])
            p.op("act", lambda e, foc=foc, b=bank: e.activation(
                tg_sb[:, foc, :], b[:, 0:512], AF.Tanh, scale=0.5),
                reads=[bk], writes=["tg"])

        if stop == "A_g":
            return _finish_early()
        if i + 1 < NT_A:
            lnA(i + 1)
        for pr in range(4):
            p.op("dve", lambda e, pr=pr: e.tensor_tensor(
                khT_pair[:].rearrange("p a (g c) -> p a g c", c=CH),
                kT[:, :, pr * 128:(pr + 1) * 128].rearrange(
                    "p a (g c) -> p a g c", c=CH),
                ACbf[:, :, 2 * pr:2 * pr + 2].rearrange("p a g -> p a g ()")
                .broadcast_to([128, NFC, 2, CH]), ALU.mult),
                reads=["kT", "AC"], writes=["xb0", "xb1"])
            p.op("sp", lambda e, pr=pr: e.dma_start(
                khtok_tile[:, pr, :].rearrange("p (a b) -> p a b", b=128),
                khT_pair[:], transpose=True),
                reads=["xb0", "xb1"], writes=["Agrp"], dma=True)

        for gc in range(NGC):
            gpar = gc % 2
            sub = gc // 2
            sbase = gpar * 64
            tsl = slice(gc * CH, (gc + 1) * CH)
            p.op("act", lambda e: e.activation(
                Sbf[:].rearrange("p a c -> p (a c)"),
                S_sb[:].rearrange("p a c -> p (a c)"), AF.Copy),
                reads=["S"], writes=["ln.t0"])
            for hp in range(2):
                for fc in range(NFC):
                    p.op("pe", lambda e, hp=hp, fc=fc, sbase=sbase, tsl=tsl: e.matmul(
                        pbPT[sbase:sbase + 64, hp * 512 + fc * 64:hp * 512 + (fc + 1) * 64],
                        kT[hp * 64:(hp + 1) * 64, fc, tsl],
                        qT[hp * 64:(hp + 1) * 64, fc, tsl],
                        start=(fc == 0), stop=(fc == NFC - 1),
                        tile_position=(hp * 64, sbase)),
                        reads=["kT", "qT"],
                        wclaim=(("PT",) if (hp == 0 and fc == 0) else ()),
                        writes=(["PT"] if (hp == 1 and fc == NFC - 1) else []),
                        inc=(fc == NFC - 1))
            p.op("dve", lambda e, sbase=sbase: e.tensor_tensor(
                ptm[sbase:sbase + 64, :, :, :],
                pbPT[sbase:sbase + 64, :].rearrange("p (b f c) -> p b f c", b=2, c=CH),
                mask[sbase:sbase + 64, :].rearrange("p c -> p () () c").broadcast_to(
                    [64, 2, NFC, CH]), ALU.mult),
                reads=["PT", "mask"], writes=["xq0", "xq1"])
            for hp in range(2):
                for fc in range(NFC):
                    hh = 2 * fc + hp
                    p.op("pe", lambda e, hh=hh, hp=hp, fc=fc, sbase=sbase, sub=sub:
                         e.matmul(
                        pbO[hp * 64:(hp + 1) * 64, fc * 64:(fc + 1) * 64],
                        v_tok[sbase:sbase + 64, sub, hh * 64:(hh + 1) * 64],
                        ptm[sbase:sbase + 64, hp, fc, :],
                        start=(fc == 0), stop=(fc == NFC - 1),
                        tile_position=(sbase, hp * 64)),
                        reads=["v", "xq0", "xq1"],
                        wclaim=(("O",) if (hp == 0 and fc == 0) else ()),
                        writes=(["O"] if (hp == 1 and fc == NFC - 1) else []),
                        inc=(fc == NFC - 1))
            for hp in range(2):
                for fc in range(NFC):
                    p.op("pe", lambda e, hp=hp, fc=fc, tsl=tsl: e.matmul(
                        pbR0[hp * 64:(hp + 1) * 64, fc * 64:(fc + 1) * 64],
                        Sbf[hp * 64:(hp + 1) * 64, fc, :],
                        qT[hp * 64:(hp + 1) * 64, fc, tsl],
                        start=(fc == 0), stop=(fc == NFC - 1),
                        tile_position=(hp * 64, hp * 64)),
                        reads=["ln.t0", "qT"],
                        wclaim=(("R0",) if (hp == 0 and fc == 0) else ()),
                        writes=(["R0"] if (hp == 1 and fc == NFC - 1) else []),
                        inc=(fc == NFC - 1))
            p.op("act", lambda e: e.activation(
                obf[:].rearrange("p a c -> p (a c)"), pbO[:, 0:512], AF.Copy),
                reads=["O"], writes=["ln.t1"])
            p.op("dve", lambda e: e.tensor_tensor(
                obf[:].rearrange("p a c -> p (a c)"),
                obf[:].rearrange("p a c -> p (a c)"), pbR0[:, 0:512], ALU.add),
                reads=["ln.t1", "R0"], writes=["ln.t1"])
            p.op("dve", lambda e, tsl=tsl: e.scalar_tensor_tensor(
                og[:, :, tsl], tg_sb[:, :, tsl], 1.0, obf[:, :, :],
                ALU.add, ALU.mult), reads=["tg", "ln.t1"], writes=["invA"])
            p.op("pool", lambda e, tsl=tsl: e.tensor_tensor(
                sqs[:, :, :], og[:, :, tsl], og[:, :, tsl], ALU.mult),
                reads=["invA"], writes=["ln.t0"])
            for fc in range(NFC):
                p.op("pe", lambda e, fc=fc, gc=gc: e.matmul(
                    pbST[0:16, gc * 64:(gc + 1) * 64],
                    blkR[:, fc * 16:(fc + 1) * 16],
                    sqs[:, fc, :], start=(fc == 0), stop=(fc == NFC - 1)),
                    reads=["ln.t0", "blkR"], wclaim=(("ST",) if fc == 0 else ()),
                    writes=(["ST"] if fc == NFC - 1 else []), inc=(fc == NFC - 1))
            for hp in range(2):
                for fc in range(NFC):
                    hh = 2 * fc + hp
                    p.op("pe", lambda e, hh=hh, hp=hp, fc=fc, sbase=sbase, sub=sub:
                         e.matmul(
                        pbO[hp * 64:(hp + 1) * 64, fc * 64:(fc + 1) * 64],
                        khtok_tile[sbase:sbase + 64, sub, hh * 64:(hh + 1) * 64],
                        v_tok[sbase:sbase + 64, sub, hh * 64:(hh + 1) * 64],
                        start=(fc == 0), stop=(fc == NFC - 1),
                        tile_position=(sbase, hp * 64)),
                        reads=["Agrp", "v"],
                        wclaim=(("O",) if (hp == 0 and fc == 0) else ()),
                        writes=(["O"] if (hp == 1 and fc == NFC - 1) else []),
                        inc=(fc == NFC - 1))
            p.op("pool", lambda e, gc=gc: e.tensor_tensor(
                S_sb[:, :, :], S_sb[:, :, :],
                ACf32[:].rearrange("p (a g) -> p a g", g=NGC)[:, :, gc:gc + 1]
                .broadcast_to([128, NFC, HD]), ALU.mult),
                reads=["S", "ACf"], writes=["S"])
            p.op("dve", lambda e: e.tensor_tensor(
                S_sb[:].rearrange("p a c -> p (a c)"),
                S_sb[:].rearrange("p a c -> p (a c)"), pbO[:, 0:512], ALU.add),
                reads=["S", "O"], writes=["S"])

        p.op("dve", lambda e: e.tensor_scalar(
            rst[0:16, 0:512], pbST[0:16, 0:512], 1.0 / HD, 4 * EPS, ALU.mult, ALU.add),
            reads=["ST"], writes=["rms.v"])
        rsqrt_via_sqrt("rms.", rs1[0:16, 0:512], rst[0:16, 0:512], rs0[0:16, 0:512], rs0[0:16, 0:512], 16, 512)
        p.op("dve", lambda e: e.tensor_copy(rbf[:], rs1[0:16, 0:512]),
             reads=["rms.r"], writes=["rbf"])
        for fc in range(NFC):
            pa = fc % 2
            bank, bk = (pbR0, "R0") if pa == 0 else (pbR1, "R1")
            p.op("pe", lambda e, fc=fc, b=bank: e.matmul(
                b[:, 0:512], blk2[:, fc * 128:(fc + 1) * 128], rbf[0:16, :],
                start=True, stop=True),
                reads=["rbf", "blk2"], writes=[bk])
            p.op("act", lambda e, b=bank: e.activation(
                rbc[:, :], b[:, 0:512], AF.Copy), reads=[bk], writes=["rbc"])
            p.op("dve", lambda e, fc=fc: e.scalar_tensor_tensor(
                og[:, fc, :], og[:, fc, :], nw_col[:, fc:fc + 1], rbc[:, :],
                ALU.mult, ALU.mult), reads=["invA", "nw", "rbc"], writes=["invA"])
        for foc in range(NFC):
            pa = foc % 2
            bank, bk = (pbR0, "R0") if pa == 0 else (pbR1, "R1")
            for kc in range(NFC):
                p.op("pe", lambda e, foc=foc, kc=kc, b=bank: e.matmul(
                    b[:, 0:512], wo[:, kc, foc * 128:(foc + 1) * 128], og[:, kc, :],
                    start=(kc == 0), stop=(kc == NFC - 1)),
                    reads=["invA", "wo"], wclaim=((bk,) if kc == 0 else ()),
                    writes=([bk] if kc == NFC - 1 else []), inc=(kc == NFC - 1))
            p.op("dve", lambda e, foc=foc, b=bank, xb=xb: e.scalar_tensor_tensor(
                xb[:, foc, :], b[:, 0:512], colv[:, 16 + foc:16 + foc + 1],
                xb[:, foc, :], ALU.mult, ALU.add),
                reads=[bk, "colv", xk], writes=[xk])
        dst = xmid[:, i * 512:(i + 1) * 512].rearrange("(fc p) t -> p fc t", p=128)
        p.op("sp", lambda e, d=dst, xb=xb: e.dma_start(d, xb[:]), reads=[xk], dma=True)
        nslice = (len(prep_tasks) + NT_A - 1 - i) // max(1, NT_A - i)
        for _ in range(min(nslice, len(prep_tasks))):
            prep_tasks.pop(0)()

    while prep_tasks:
        prep_tasks.pop(0)()

    p.barrier()

    for kc in range(NFC):
        p.op("act", lambda e, kc=kc: e.dma_start(
            wgT[:, kc, :], sgate[:, kc * 128:(kc + 1) * 128], transpose=True),
            writes=["wgT"], dma=True)
        p.op("act", lambda e, kc=kc: e.dma_start(
            wuT[:, kc, :], sup[:, kc * 128:(kc + 1) * 128], transpose=True),
            writes=["wuT"], dma=True)
    for ic in range(NIC):
        p.op("act", lambda e, ic=ic: e.dma_start(
            wdT[:, ic, :], sdown[:, ic * 128:(ic + 1) * 128], transpose=True),
            writes=["wdT"], dma=True)

    nch_list = []
    n0 = 0
    while n0 < FFI:
        nch_list.append((n0, min(512, FFI - n0)))
        n0 += 512

    for bp in range(2):
        p.op("pool", lambda e, bp=bp: e.memset(w1b[bp][:], 0.0),
             reads=[], writes=[f"hm{bp}"])

    xmbs = [xm, xm2]

    def lnb(i):
        xmb = xmbs[i % 2]
        xmk = f"xm{i % 2}"
        src = xmid[:, i * 256:(i + 1) * 256].rearrange("(fc p) t -> p fc t", p=128)
        p.op("sp", lambda e, s=src, xmb=xmb: e.dma_start(xmb[:], s),
             writes=[xmk], dma=True)
        layernorm(xmb[:], xmk, 256, 24,
                  lambda fc: h2_sb[:, :, fc, :],
                  "h2", xbs_b, xqs_b, t0b, t1b,
                  tin=lambda t: t[:, 0:256].rearrange("p (s b) -> p s b", b=128))

    lnb(0)
    for i in range(NT_B):
        xmb = xmbs[i % 2]
        xmk = f"xm{i % 2}"

        def h2_in(sub):
            ssl = slice(sub * 128, (sub + 1) * 128)
            p.op("sp", lambda e, sub=sub: e.dma_start(
                h2tok[:].rearrange("p (a b) -> p a b", b=128),
                h2_sb[:, sub, :, :], transpose=True),
                reads=["h2"], writes=["h2tok"], dma=True)
        def h2_lad():
            _sk = []
            for half in range(2):
                hsl = slice(half * 512, (half + 1) * 512)
                s = OpSink()
                ladder_quant("dve", "h2tok", h2tok[:, hsl], "h2tok",
                             h2tok[:, hsl], 512, QSC_B[half], f"qb{half}.",
                             sink=s)
                _sk.append(s)
            zip_emit(*_sk)
        def h2_out(sub):
            p.op("act", lambda e, sub=sub: e.dma_start(
                h2qT[:, sub, :, :],
                h2tok[:].rearrange("p (a b) -> p a b", b=128), transpose=True),
                writes=["h2tok", "h2qT"], dma=True)
        def nch_block(sub):
            for pj in range(0, len(nch_list), 2):
                pcs = []
                for bp in range(2):
                    nn0, nwd = nch_list[pj + bp]
                    gb_, gk = (pbR0, "R0") if bp == 0 else (pbO, "O")
                    ub_, uk = (pbR1, "R1") if bp == 0 else (pbPT, "PT")
                    pcs.append((bp, nn0, nwd, gb_, gk, ub_, uk))
                for bp, nn0, nwd, gb_, gk, ub_, uk in pcs:
                    for kc in range(NFC):
                        p.op("pe", lambda e, kc=kc, nn0=nn0, nwd=nwd, sub=sub,
                             gb_=gb_: e.matmul(
                            gb_[:, 0:nwd], h2qT[:, sub, kc, :],
                            wgT[:, kc, nn0:nn0 + nwd],
                            start=(kc == 0), stop=(kc == NFC - 1)),
                            reads=["h2qT", "wgT"],
                            wclaim=((gk,) if kc == 0 else ()),
                            writes=([gk] if kc == NFC - 1 else []),
                            inc=(kc == NFC - 1))
                    for kc in range(NFC):
                        p.op("pe", lambda e, kc=kc, nn0=nn0, nwd=nwd, sub=sub,
                             ub_=ub_: e.matmul(
                            ub_[:, 0:nwd], h2qT[:, sub, kc, :],
                            wuT[:, kc, nn0:nn0 + nwd],
                            start=(kc == 0), stop=(kc == NFC - 1)),
                            reads=["h2qT", "wuT"],
                            wclaim=((uk,) if kc == 0 else ()),
                            writes=([uk] if kc == NFC - 1 else []),
                            inc=(kc == NFC - 1))
                for bp, nn0, nwd, gb_, gk, ub_, uk in pcs:
                    p.op("act", lambda e, nwd=nwd, gb_=gb_: e.activation(
                        tanb[:, 0:nwd], gb_[:, 0:nwd], AF.Tanh, scale=0.5),
                        reads=[gk], writes=["tanb"])
                    p.op("act", lambda e, nwd=nwd, gb_=gb_: e.activation(
                        gtbf[:, 0:nwd], gb_[:, 0:nwd], AF.Copy, scale=0.5),
                        reads=[gk], writes=["gtbf"])
                    p.op("act", lambda e, nwd=nwd, ub_=ub_: e.activation(
                        upbf[:, 0:nwd], ub_[:, 0:nwd], AF.Copy),
                        reads=[uk], writes=["upbf"])
                    p.op("dve", lambda e, nwd=nwd, bp=bp: e.scalar_tensor_tensor(
                        w1b[bp][:, 0:nwd], tanb[:, 0:nwd], 1.0, upbf[:, 0:nwd],
                        ALU.add, ALU.mult), reads=["tanb", "upbf"],
                        writes=[f"hm{bp}"])
                    p.op("dve", lambda e, nwd=nwd, bp=bp: e.tensor_mul(
                        w1b[bp][:, 0:nwd], gtbf[:, 0:nwd], w1b[bp][:, 0:nwd]),
                        reads=["gtbf", f"hm{bp}"], writes=[f"hm{bp}"])
                sks = []
                for bp, nn0, nwd, gb_, gk, ub_, uk in pcs:
                    s = OpSink()
                    ladder_quant("dve", f"hm{bp}", w1b[bp][:, 0:nwd], f"hm{bp}",
                                 w1b[bp][:, 0:nwd], nwd, QSC_B[bp], f"qb{bp}.",
                                 sink=s)
                    sks.append(s)
                zip_emit(*sks)
                for bp, nn0, nwd, gb_, gk, ub_, uk in pcs:
                    nic0 = nn0 // 128
                    nicw = nwd // 128
                    p.op("act", lambda e, nic0=nic0, nicw=nicw, nwd=nwd, sub=sub,
                         bp=bp: e.dma_start(
                        hmqT[:, sub, nic0:nic0 + nicw, :],
                        w1b[bp][:, 0:nwd].rearrange("p (a b) -> p a b", b=128),
                        transpose=True),
                        writes=[f"hm{bp}", "hmqT"], dma=True)
        if i == 0:
            h2_in(0)
        h2_lad()
        h2_out(0)
        h2_in(1)
        h2_lad()
        nch_block(0)
        h2_out(1)
        nch_block(1)
        if i + 1 < NT_B:
            lnb(i + 1)
            h2_in(0)
        for foc in range(NFC):
            pa = foc % 2
            bank, bk = (pbR0, "R0") if pa == 0 else (pbR1, "R1")
            for ic in range(NIC):
                p.op("pe", lambda e, foc=foc, ic=ic, b=bank: e.matmul(
                    b[:, 0:256], wdT[:, ic, foc * 128:(foc + 1) * 128],
                    hmqT[:, :, ic, :], start=(ic == 0), stop=(ic == NIC - 1)),
                    reads=["hmqT", "wdT"], wclaim=((bk,) if ic == 0 else ()),
                    writes=([bk] if ic == NIC - 1 else []), inc=(ic == NIC - 1))
            p.op("dve", lambda e, foc=foc, b=bank, xmb=xmb: e.scalar_tensor_tensor(
                xmb[:, foc, :], b[:, 0:256], colv[:, 40 + foc:40 + foc + 1],
                xmb[:, foc, :], ALU.mult, ALU.add),
                reads=[bk, "colv", xmk], writes=[xmk])
        dst = y_fm[:, i * 256:(i + 1) * 256].rearrange("(fc p) t -> p fc t", p=128)
        p.op("sp", lambda e, d=dst, xmb=xmb: e.dma_start(d, xmb[:]),
             reads=[xmk], dma=True)

    p.barrier()
    p.emit(nc)
    return nc


_NC_CACHE = {}


def _get_nc(ntok):
    if ntok not in _NC_CACHE:
        _NC_CACHE[ntok] = build_nc(ntok)
    return _NC_CACHE[ntok]


def _consts():
    mask128 = np.zeros((128, CH), np.float32)
    for pp in range(128):
        mask128[pp, pp % 64:] = 1.0
    blkR16 = np.zeros((128, 128), ml_dtypes.bfloat16)
    sel16 = np.zeros((16, 1024), ml_dtypes.bfloat16)
    for fc in range(8):
        for pp in range(128):
            j = 2 * fc + (1 if pp >= 64 else 0)
            blkR16[pp, fc * 16 + j] = 1
            sel16[j, fc * 128 + pp] = 1
    return {
        "identb": np.eye(128, dtype=ml_dtypes.bfloat16),
        "mask128": mask128,
        "blkR16": blkR16,
        "sel16": sel16,
        "ones1": np.ones((128, 1), ml_dtypes.bfloat16),
        "onesb": np.ones((1, 128), np.float32),
        "zero512": np.zeros((128, 512), np.float32),
        "zero64b": np.zeros((128, CH), ml_dtypes.bfloat16),
    }


def kernel(L0_attn_x, L0_attn_Wq, L0_attn_Wk, L0_attn_Wv, L0_attn_Wo,
           L0_attn_Wf1, L0_attn_Wf2, L0_attn_Wg1, L0_attn_Wg2,
           L0_attn_norm_weight, L0_ffn_w_gate, L0_ffn_w_up, L0_ffn_w_down,
           L0_c, L0_wmod):
    x = np.asarray(L0_attn_x)
    B, ntok, d = x.shape
    assert d == D
    nc = _get_nc(ntok)
    shared = {
        "Wq": np.ascontiguousarray(L0_attn_Wq, np.float32),
        "Wk": np.ascontiguousarray(L0_attn_Wk, np.float32),
        "Wv": np.ascontiguousarray(L0_attn_Wv, np.float32),
        "Wo": np.ascontiguousarray(L0_attn_Wo, np.float32),
        "Wf1": np.ascontiguousarray(L0_attn_Wf1, np.float32),
        "Wf2": np.ascontiguousarray(L0_attn_Wf2, np.float32),
        "Wg1": np.ascontiguousarray(L0_attn_Wg1, np.float32),
        "Wg2": np.ascontiguousarray(L0_attn_Wg2, np.float32),
        "wmod": np.ascontiguousarray(L0_wmod, np.float32),
        "w_gate": np.ascontiguousarray(L0_ffn_w_gate, np.float32),
        "w_up": np.ascontiguousarray(L0_ffn_w_up, np.float32),
        "w_down": np.ascontiguousarray(L0_ffn_w_down, np.float32),
        "nw_col": np.ascontiguousarray(
            np.asarray(L0_attn_norm_weight, np.float32).reshape(NFC, 128).T),
    }
    shared.update(_consts())
    cc = np.asarray(L0_c, np.float32)
    in_maps = []
    for b in range(B):
        m = dict(shared)
        m["xT"] = np.ascontiguousarray(x[b].T.astype(np.float32, copy=False))
        m["c_col"] = np.ascontiguousarray(cc[b].reshape(NFC, 128).T)
        in_maps.append(m)
    res = run_bass_kernel_spmd(nc, in_maps, core_ids=list(range(B)))
    out = np.stack([np.ascontiguousarray(res.results[b]["y_fm"].T)
                    for b in range(B)], axis=0)
    return out
